# version 03
# baseline (speedup 1.0000x reference)
import contextlib
import numpy as np
import concourse.bass as bass
import concourse.mybir as mybir
from concourse.bass_utils import run_bass_kernel_spmd

F32 = mybir.dt.float32
BF16 = mybir.dt.bfloat16
AF = mybir.ActivationFunctionType
ALU = mybir.AluOpType
AX = mybir.AxisListType

NCORES = 8
NSEQ = 4
S = 2048
D = 1024
T = NSEQ * S
DH = 4096
NEXP = 8
EPS = 1e-5
ENGS = ("tensor", "vector", "scalar", "sync", "gpsimd")
P4_DEBUG = [False]


class P:

    def __init__(self, nc, ctx):
        self.nc = nc
        self.ctx = ctx
        self.streams = {e: [] for e in ENGS}
        self.sems = {}
        self.cnt = {}
        self.nbar = 0

    def sem(self, name):
        if name not in self.sems:
            self.sems[name] = self.ctx.enter_context(self.nc.semaphore(name))
            self.cnt[name] = 0
        return self.sems[name]

    def op(self, eng, fn, wait=(), inc=(), dma=False, ss=None):
        wait = list(wait)
        if ss is not None:
            wait.append((ss, self.c(ss)))
        for s, _ in wait:
            self.sem(s)
        incs = []
        for s in inc:
            self.sem(s)
            self.cnt[s] += 16 if dma else 1
            incs.append((self.sems[s], 16 if dma else 1))
        waits = [(self.sems[s], v) for (s, v) in wait]
        def run(e, waits=waits, fn=fn, incs=incs):
            for h, v in waits:
                e.wait_ge(h, v)
            inst = fn(e)
            for h, d in incs:
                inst = inst.then_inc(h, d)
        self.streams[eng].append(run)
        return self.cnt[inc[0]] if inc else None

    def c(self, s):
        self.sem(s)
        return self.cnt[s]

    def barrier(self):
        self.nbar += 1
        b = self.sem("bar")
        tgt = 5 * self.nbar
        for e in ENGS:
            def run(eng, b=b, tgt=tgt):
                eng.sem_inc(b, 1)
                eng.wait_ge(b, tgt)
            self.streams[e].append(run)
        self.cnt["bar"] = tgt


def _chunked(ap):
    return ap.rearrange("(c p) t -> p c t", p=128)


def emit_p2(p, io, g, TB=512, NT=None):
    nc = p.nc
    if NT is None:
        NT = T // TB
    NH = TB // 512
    pre = "p2"
    with contextlib.ExitStack() as ctx:
        sb = lambda n, sh, dt: ctx.enter_context(nc.sbuf_tensor(pre + n, sh, dt))
        psD = ctx.enter_context(nc.psum_tensor(pre + "psD", [128, 2048], F32))
        psA = ctx.enter_context(nc.psum_tensor(pre + "psA", [128, 512], F32))
        psB = ctx.enter_context(nc.psum_tensor(pre + "psB", [128, 512], F32))
        psC = ctx.enter_context(nc.psum_tensor(pre + "psC", [128, 512], F32))

        x1b = sb("x1b", [128, 8, TB], BF16)
        t1 = sb("t1", [128, 32, TB], BF16)
        t2 = sb("t2", [128, 32, TB], BF16)
        wstrip = [sb(f"ws{i}", [128, 8192], BF16) for i in range(2)]
        wv16 = [w.rearrange("p (a b) -> p a b", a=16) for w in wstrip]
        wv32 = [w.rearrange("p (a b) -> p a b", a=32) for w in wstrip]
        wv32c = [w.rearrange("p (a b) -> p a b", a=64) for w in wstrip]
        s1s = sb("s1s", [128, 32], F32)
        s2s = sb("s2s", [128, 32], F32)
        hs1s = sb("hs1s", [128, 32], F32)
        hs2s = sb("hs2s", [128, 32], F32)
        ws3s = sb("ws3s", [128, 8], F32)
        silu = [sb(f"sil{i}", [128, 512], F32) for i in range(2)]
        sbf = sb("sbf", [128, 8, TB], BF16)
        sqbf = sb("sqbf", [128, TB], BF16)
        Asc = sb("Asc", [128, TB], F32)
        Bsc = sb("Bsc", [128, TB], F32)
        tmp = sb("tmp", [128, TB], F32)
        x1f = [sb(f"x1f{i}", [128, TB], F32) for i in range(2)]
        x2f = [sb(f"x2f{i}", [128, TB], F32) for i in range(2)]

        for name, tile in [("w1_s", s1s), ("w2_s", s2s), ("h1_s", hs1s),
                           ("h2_s", hs2s), ("w3_s", ws3s)]:
            ap = io[name]
            p.op("sync", lambda e, t=tile, a=ap: e.dma_start(out=t[:], in_=a[:]),
                 inc=["p2s"], dma=True)
        scales_ld = p.c("p2s")

        strip_free = [("p2t", 0), ("p2t", 0)]
        silu_cons = [0, 0]
        sq_prev = [0]

        def strips_load(wname, view, par, nk, width, col0, row0=0):
            ap = _chunked(io[wname])
            ws, wv = strip_free[par]
            p.op("gpsimd",
                 lambda e, view=view, ap=ap, nk=nk, width=width, col0=col0, row0=row0:
                     e.dma_start(out=view[:, row0:row0 + nk, 0:width],
                                 in_=ap[:, 0:nk, col0:col0 + width]),
                 wait=[(ws, wv)] if wv > 0 else [], inc=["p2g"], dma=True)
            return p.c("p2g")

        for j in range(NT):
            jc = slice(j * TB, (j + 1) * TB)
            x1b_guard = [("p2t", p.c("p2t"))] if j > 0 else []
            p.op("gpsimd",
                 lambda e, jc=jc: e.dma_start(out=x1b[:], in_=_chunked(io["x1"])[:, :, jc]),
                 wait=x1b_guard, inc=["p2g"], dma=True)
            x1b_ld = p.c("p2g")

            for mg in range(8):
                par = mg % 2
                l1 = strips_load("w1T", wv16[par], par, 8, 512, mg * 512)
                strip_free[par] = (strip_free[par][0], 0)
                l2 = strips_load("w2T", wv16[par], par, 8, 512, mg * 512, row0=8)
                for m in range(4):
                    mi = mg * 4 + m
                    for nh in range(NH):
                        nhs = slice(nh * 512, (nh + 1) * 512)
                        for k in range(8):
                            p.op("tensor",
                                 lambda e, k=k, par=par, m=m, nh=nh, jc=jc, nhs=nhs:
                                     e.matmul(psA[:], wv16[par][:, k, m * 128:(m + 1) * 128],
                                              x1b[:, k, nhs],
                                              start=(k == 0), stop=(k == 7),
                                              skip_group_check=True),
                                 wait=([(("p2g"), l2), ("p2g", x1b_ld),
                                        ("p2a", p.c("p2a"))] if k == 0 else []),
                                 inc=(["p2t"] if k == 7 else []))
                        for k in range(8):
                            p.op("tensor",
                                 lambda e, k=k, par=par, m=m, nh=nh, nhs=nhs:
                                     e.matmul(psB[:], wv16[par][:, 8 + k, m * 128:(m + 1) * 128],
                                              x1b[:, k, nhs],
                                              start=(k == 0), stop=(k == 7),
                                              skip_group_check=True),
                                 wait=([("p2v", p.c("p2v"))] if k == 0 else []),
                                 inc=(["p2t"] if k == 7 else []))
                        tmm = p.c("p2t")
                        sbuf_i = (mi * NH + nh) % 2
                        p.op("scalar",
                             lambda e, sbuf_i=sbuf_i, mi=mi:
                                 e.activation(silu[sbuf_i][:], psA[:], AF.Silu,
                                              scale=s1s[:, mi:mi + 1]),
                             wait=[("p2t", tmm), ("p2s", scales_ld),
                                   ("p2v", silu_cons[sbuf_i])], inc=["p2a"])
                        amm = p.c("p2a")
                        p.op("vector",
                             lambda e, sbuf_i=sbuf_i, mi=mi, nhs=nhs:
                                 e.scalar_tensor_tensor(t1[:, mi, nhs], psB[:],
                                                        s2s[:, mi:mi + 1], silu[sbuf_i][:],
                                                        ALU.mult, ALU.mult),
                             wait=[("p2a", amm)], inc=["p2v"])
                        silu_cons[sbuf_i] = p.c("p2v")
                strip_free[par] = ("p2v", p.c("p2v"))

            for (wname, scl, src, dst) in [("h1T", hs1s, t1, t2), ("h2T", hs2s, t2, t1)]:
                for mg in range(16):
                    par = mg % 2
                    lh = strips_load(wname, wv32[par], par, 32, 256, mg * 256)
                    for m in range(2):
                        mi = mg * 2 + m
                        for nh in range(NH):
                            nhs = slice(nh * 512, (nh + 1) * 512)
                            ps = psA if (mi + nh) % 2 == 0 else psB
                            for k in range(32):
                                p.op("tensor",
                                     lambda e, k=k, par=par, m=m, nhs=nhs, ps=ps, src=src:
                                         e.matmul(ps[:], wv32[par][:, k, m * 128:(m + 1) * 128],
                                                  src[:, k, nhs],
                                                  start=(k == 0), stop=(k == 31),
                                                  skip_group_check=True),
                                     wait=([("p2g", lh), ("p2v", p.c("p2v")),
                                            ("p2a", p.c("p2a"))] if k == 0 else []),
                                     inc=(["p2t"] if k == 31 else []))
                            tmm = p.c("p2t")
                            p.op("scalar",
                                 lambda e, mi=mi, nhs=nhs, ps=ps, dst=dst, scl=scl:
                                     e.activation(dst[:, mi, nhs], ps[:], AF.Gelu,
                                                  scale=scl[:, mi:mi + 1]),
                                 wait=[("p2t", tmm)], inc=["p2a"])
                    strip_free[par] = ("p2a", p.c("p2a"))

            for m in range(8):
                par = m % 2
                lw = strips_load("w3T", wv32c[par], par, 32, 128, m * 128)
                p.op("sync",
                     lambda e, m=m, jc=jc, par=par:
                         e.dma_start(out=x1f[par][:], in_=io["x1"][m * 128:(m + 1) * 128, jc]),
                     wait=[("p2v", max(0, p.c("p2v")))] if (j > 0 or m >= 2) else [],
                     inc=["p2s"], dma=True)
                lx = p.c("p2s")
                for nh in range(NH):
                    nhs = slice(nh * 512, (nh + 1) * 512)
                    for k in range(32):
                        p.op("tensor",
                             lambda e, k=k, par=par, nhs=nhs:
                                 e.matmul(psC[:], wv32c[par][:, k, 0:128], t1[:, k, nhs],
                                          start=(k == 0), stop=(k == 31),
                                          skip_group_check=True),
                             wait=([("p2g", lw), ("p2v", p.c("p2v"))] if k == 0 else []),
                             inc=(["p2t"] if k == 31 else []))
                    tmm = p.c("p2t")
                    p.op("vector",
                         lambda e, m=m, nhs=nhs, par=par:
                             e.scalar_tensor_tensor(sbf[:, m, nhs], psC[:],
                                                    ws3s[:, m:m + 1], x1f[par][:, nhs],
                                                    ALU.mult, ALU.add),
                         wait=[("p2s", lx), ("p2t", tmm)], inc=["p2v"])
                strip_free[par] = ("p2v", p.c("p2v"))
                svm = p.c("p2v")
                p.op("scalar",
                     lambda e, m=m: e.activation(sqbf[:], sbf[:, m, :], AF.Square),
                     wait=[("p2v", svm), ("p2t", sq_prev[0])], inc=["p2a"])
                sqm = p.c("p2a")
                for nh in range(NH):
                    nhs = slice(nh * 512, (nh + 1) * 512)
                    p.op("tensor",
                         lambda e, m=m, nh=nh, nhs=nhs:
                             e.matmul(psD[:, nh * 512:nh * 512 + 512], g["ones128"][:],
                                      sbf[:, m, nhs], start=(m == 0), stop=(m == 7),
                                      skip_group_check=True),
                         wait=[("p2v", svm)] + ([("p2v", p.c("p2v"))] if m == 0 and nh == 0 else []),
                         inc=(["p2t"] if m == 7 and nh == NH - 1 else []))
                    p.op("tensor",
                         lambda e, m=m, nh=nh, nhs=nhs:
                             e.matmul(psD[:, 1024 + nh * 512:1024 + nh * 512 + 512],
                                      g["ones128"][:], sqbf[:, nhs], start=(m == 0),
                                      stop=(m == 7), skip_group_check=True),
                         wait=[("p2a", sqm)],
                         inc=["p2t"])
                    sq_prev[0] = p.c("p2t")
            tstat = p.c("p2t")
            p.op("vector", lambda e: e.tensor_scalar(Asc[:], psD[:, 0:TB], 1.0 / D, None, ALU.mult),
                 wait=[("p2t", tstat)], inc=["p2v"])
            p.op("vector", lambda e: e.tensor_scalar(tmp[:], psD[:, 1024:1024 + TB], 1.0 / D, None, ALU.mult),
                 inc=["p2v"])
            p.op("vector", lambda e: e.tensor_tensor(Bsc[:], Asc[:], Asc[:], ALU.mult), inc=["p2v"], ss="p2v")
            p.op("vector", lambda e: e.tensor_sub(tmp[:], tmp[:], Bsc[:]), inc=["p2v"], ss="p2v")
            p.op("vector", lambda e: e.tensor_scalar(tmp[:], tmp[:], EPS, None, ALU.add), inc=["p2v"], ss="p2v")
            p.op("vector", lambda e: e.reciprocal(tmp[:], tmp[:]), inc=["p2v"], ss="p2v")
            mean_done = p.c("p2v")
            p.op("scalar", lambda e: e.activation(tmp[:], tmp[:], AF.Sqrt),
                 wait=[("p2v", mean_done)], inc=["p2a"])
            rstd_done = p.c("p2a")
            p.op("vector", lambda e: e.scalar_tensor_tensor(Bsc[:], Asc[:], -1.0, tmp[:], ALU.mult, ALU.mult),
                 wait=[("p2a", rstd_done)], inc=["p2v"], ss="p2v")
            for m in range(8):
                buf = x2f[m % 2]
                st_guard = [("p2s", p.c("p2s"))] if m >= 2 else []
                p.op("vector",
                     lambda e, m=m, buf=buf: e.tensor_tensor(buf[:], sbf[:, m, :], tmp[:], ALU.mult),
                     wait=st_guard, inc=["p2v"], ss="p2v")
                p.op("vector",
                     lambda e, m=m, buf=buf: e.tensor_add(buf[:], buf[:], Bsc[:]),
                     inc=["p2v"], ss="p2v")
                vd = p.c("p2v")
                p.op("sync",
                     lambda e, m=m, jc=jc, buf=buf:
                         e.dma_start(out=io["x2"][m * 128:(m + 1) * 128, jc], in_=buf[:]),
                     wait=[("p2v", vd)], inc=["p2s"], dma=True)
        p.op("sync", lambda e: None, wait=[("p2s", p.c("p2s"))])
    p.barrier()


def emit_p1(p, io, g, NSQ=None):
    nc = p.nc
    nseq = NSEQ if NSQ is None else NSQ
    pre = "p1"
    with contextlib.ExitStack() as ctx:
        sb = lambda n, sh, dt: ctx.enter_context(nc.sbuf_tensor(pre + n, sh, dt))
        psA = ctx.enter_context(nc.psum_tensor(pre + "psA", [128, 512], F32))
        psC = ctx.enter_context(nc.psum_tensor(pre + "psC", [128, 512], F32))
        psD = ctx.enter_context(nc.psum_tensor(pre + "psD", [128, 2048], F32))
        psU = [ctx.enter_context(nc.psum_tensor(pre + f"psU{i}", [128, 1024], BF16))
               for i in range(2)]

        warena = sb("warena", [128, 8, 3072], BF16)
        wq = warena[:, :, 0:1024]
        wk = warena[:, :, 1024:2048]
        wv = warena[:, :, 2048:3072]
        wo = warena[:, :, 0:1024]
        bqs = sb("bqs", [128, 8], F32)
        bks = sb("bks", [128, 8], F32)
        bvr = sb("bvr", [1, 1024], BF16)
        bos = sb("bos", [128, 8], F32)
        xsb_u = sb("xsb", [128, 8, 512], BF16)
        x0b = xsb_u
        Q = sb("Q", [128, 8, 2048], BF16)
        K = sb("K", [128, 8, 2048], BF16)
        V = sb("V", [128, 16, 1024], BF16)
        AO = sb("AO", [128, 8, 2048], BF16)
        P_sb = sb("P", [128, 2048], BF16)
        PT = [sb(f"PT{i}", [128, 128], BF16) for i in range(2)]
        AOp = sb("AOp", [128, 128], BF16)
        mx = sb("mx", [128, 1], F32)
        negmx = sb("negmx", [128, 1], F32)
        sums = sb("sums", [128, 4], F32)
        rsum = sb("rsum", [128, 1], F32)
        sbf = xsb_u
        sqbf = sb("sqbf", [128, 512], BF16)
        Asc = sb("Asc", [128, 512], F32)
        Bsc = sb("Bsc", [128, 512], F32)
        tmp = sb("tmp", [128, 512], F32)
        x0f = [sb(f"x0f{i}", [128, 512], F32) for i in range(2)]
        x1f = x0f

        p.op("gpsimd", lambda e: e.dma_start(out=bvr[:], in_=io["l0_bv"][:]),
             inc=["p1g"], dma=True)
        for name, tile in [("l0_bq", bqs), ("l0_bk", bks), ("l0_bo", bos)]:
            p.op("sync", lambda e, t=tile, a=io[name]: e.dma_start(out=t[:], in_=a[:]),
                 inc=["p1s"], dma=True)
        b_ld = p.c("p1s")

        sq_prev = [0]
        for b in range(nseq):
            sb0 = b * S
            for name, tile in [("l0_wq", wq), ("l0_wk", wk), ("l0_wv", wv)]:
                p.op("gpsimd", lambda e, t=tile, a=io[name]:
                         e.dma_start(out=t[:], in_=_chunked(a)[:]),
                     wait=[("p1t", p.c("p1t"))], inc=["p1g"], dma=True)
            w_ld = p.c("p1g")
            for tb in range(4):
                tc = slice(sb0 + tb * 512, sb0 + (tb + 1) * 512)
                p.op("gpsimd",
                     lambda e, tc=tc: e.dma_start(out=x0b[:], in_=_chunked(io["x0"])[:, :, tc]),
                     wait=([("p1t", p.c("p1t")), ("p1v", p.c("p1v"))] if (b or tb) else []),
                     inc=["p1g"], dma=True)
                xld = p.c("p1g")
                for (W, dst, bias) in [(wq, Q, bqs), (wk, K, bks)]:
                    for g8 in range(8):
                        ps = psA if g8 % 2 == 0 else psC
                        for k in range(8):
                            p.op("tensor",
                                 lambda e, W=W, g8=g8, k=k, ps=ps:
                                     e.matmul(ps[:], W[:, k, g8 * 128:(g8 + 1) * 128],
                                              x0b[:, k, :], start=(k == 0), stop=(k == 7),
                                              skip_group_check=True),
                                 wait=([("p1g", max(xld, w_ld)), ("p1a", p.c("p1a"))]
                                       if k == 0 else []),
                                 inc=(["p1t"] if k == 7 else []))
                        tmm = p.c("p1t")
                        p.op("scalar",
                             lambda e, dst=dst, g8=g8, tb=tb, ps=ps, bias=bias:
                                 e.activation(dst[:, g8, tb * 512:(tb + 1) * 512], ps[:],
                                              AF.Identity, bias=bias[:, g8:g8 + 1]),
                             wait=[("p1t", tmm), ("p1s", b_ld)], inc=["p1a"])
                for ts in range(4):
                    for nh in range(2):
                        pd = psD[:, (ts % 2) * 1024 + nh * 512:(ts % 2) * 1024 + (nh + 1) * 512]
                        for k in range(8):
                            p.op("tensor",
                                 lambda e, ts=ts, nh=nh, k=k, pd=pd:
                                     e.matmul(pd, x0b[:, k, ts * 128:(ts + 1) * 128],
                                              wv[:, k, nh * 512:(nh + 1) * 512],
                                              start=(k == 0), stop=False,
                                              skip_group_check=True),
                                 wait=([("p1v", p.c("p1v"))] if k == 0 else []),
                                 inc=[])
                        p.op("tensor",
                             lambda e, nh=nh, pd=pd:
                                 e.matmul(pd, g["ones1"][0:1, 0:128],
                                          bvr[:, nh * 512:(nh + 1) * 512],
                                          start=False, stop=True, skip_group_check=True),
                             inc=["p1t"])
                        vmm = p.c("p1t")
                        p.op("vector",
                             lambda e, ts=ts, tb=tb, nh=nh, pd=pd:
                                 e.tensor_copy(V[:, tb * 4 + ts, nh * 512:(nh + 1) * 512], pd),
                             wait=[("p1t", vmm)], inc=["p1v"])
            p.op("gpsimd", lambda e: e.dma_start(out=wo[:], in_=_chunked(io["l0_wo"])[:]),
                 wait=[("p1t", p.c("p1t"))], inc=["p1g"], dma=True)
            wo_ld = p.c("p1g")
            qkv_a = p.c("p1a")
            qkv_v = p.c("p1v")

            for g8 in range(8):
                for qt in range(16):
                    keys = (qt + 1) * 128
                    nkb = (keys + 511) // 512
                    for h in range(2):
                        hp = slice(64 * h, 64 * h + 64)
                        for kb in range(nkb):
                            N = min(512, keys - kb * 512)
                            p.op("tensor",
                                 lambda e, g8=g8, qt=qt, hp=hp, kb=kb, N=N:
                                     e.matmul(psD[:, kb * 512:kb * 512 + N],
                                              Q[hp, g8, qt * 128:(qt + 1) * 128],
                                              K[hp, g8, kb * 512:kb * 512 + N],
                                              start=True, stop=True, skip_group_check=True),
                                 wait=([("p1a", max(qkv_a, p.c("p1a"))),
                                        ("p1v", p.c("p1v"))] if kb == 0 else []),
                                 inc=(["p1t"] if kb == nkb - 1 else []))
                        smm = p.c("p1t")
                        p.op("vector",
                             lambda e, keys=keys: e.tensor_add(psD[:, keys - 128:keys],
                                                               psD[:, keys - 128:keys],
                                                               g["maskf"][:]),
                             wait=[("p1t", smm)], inc=["p1v"])
                        p.op("vector",
                             lambda e, keys=keys: e.reduce_max(mx[:], psD[:, 0:keys], axis=AX.X),
                             inc=["p1v"], ss="p1v")
                        p.op("vector", lambda e: e.tensor_scalar(negmx[:], mx[:], -0.125, None,
                                                                 ALU.mult),
                             inc=["p1v"], ss="p1v")
                        exw = p.c("p1v")
                        for kb in range(nkb):
                            N = min(512, keys - kb * 512)
                            p.op("scalar",
                                 lambda e, kb=kb, N=N:
                                     e.activation(P_sb[:, kb * 512:kb * 512 + N],
                                                  psD[:, kb * 512:kb * 512 + N], AF.Exp,
                                                  bias=negmx[:], scale=0.125,
                                                  accum_out=sums[:, kb:kb + 1]),
                                 wait=([("p1v", exw), ("p1t", p.c("p1t"))] if kb == 0 else []),
                                 inc=["p1a"])
                        exd = p.c("p1a")
                        p.op("vector",
                             lambda e, nkb=nkb: e.reduce_sum(rsum[:], sums[:, 0:nkb], axis=AX.X),
                             wait=[("p1a", exd)], inc=["p1v"], ss="p1v")
                        p.op("vector", lambda e: e.reciprocal(rsum[:], rsum[:]),
                             inc=["p1v"], ss="p1v")
                        for kt in range(qt + 1):
                            pu = psU[kt % 2]
                            p.op("tensor",
                                 lambda e, kt=kt, pu=pu:
                                     e.transpose(pu[:, 0:128], P_sb[:, kt * 128:(kt + 1) * 128],
                                                 g["identb"][:]),
                                 wait=([("p1a", exd)] if kt == 0 else []) +
                                      [("p1v", p.c("p1v"))],
                                 inc=["p1t"])
                            trd = p.c("p1t")
                            p.op("vector",
                                 lambda e, kt=kt, pu=pu: e.tensor_copy(PT[kt % 2][:], pu[:, 0:128]),
                                 wait=[("p1t", trd)], inc=["p1v"])
                            ptd = p.c("p1v")
                            p.op("tensor",
                                 lambda e, kt=kt, qt=qt, g8=g8, h=h:
                                     e.matmul(psC[:, 0:64], PT[kt % 2][:],
                                              V[:, kt, (2 * g8 + h) * 64:(2 * g8 + h + 1) * 64],
                                              start=(kt == 0), stop=(kt == qt),
                                              skip_group_check=True),
                                 wait=[("p1v", max(ptd, qkv_v))],
                                 inc=(["p1t"] if kt == qt else []))
                        avd = p.c("p1t")
                        p.op("vector",
                             lambda e, h=h: e.tensor_scalar(AOp[:, h * 64:(h + 1) * 64],
                                                            psC[:, 0:64], rsum[:], None,
                                                            ALU.mult),
                             wait=[("p1t", avd)], inc=["p1v"], ss="p1v")
                    aod = p.c("p1v")
                    p.op("tensor",
                         lambda e: e.transpose(psU[0][:, 0:128], AOp[:], g["identb"][:]),
                         wait=[("p1v", aod)], inc=["p1t"])
                    aot = p.c("p1t")
                    p.op("vector",
                         lambda e, g8=g8, qt=qt: e.tensor_copy(AO[:, g8, qt * 128:(qt + 1) * 128],
                                                               psU[0][:, 0:128]),
                         wait=[("p1t", aot)], inc=["p1v"])
            attn_v = p.c("p1v")

            for nb in range(4):
                jc = slice(sb0 + nb * 512, sb0 + (nb + 1) * 512)
                for m in range(8):
                    ps = psA if m % 2 == 0 else psC
                    p.op("sync",
                         lambda e, m=m, jc=jc, par=m % 2:
                             e.dma_start(out=x0f[par][:], in_=io["x0"][m * 128:(m + 1) * 128, jc]),
                         wait=([("p1v", p.c("p1v")), ("p1s", p.c("p1s"))]
                               if (b or nb or m >= 2) else []),
                         inc=["p1s"], dma=True)
                    lx = p.c("p1s")
                    for k in range(8):
                        p.op("tensor",
                             lambda e, k=k, m=m, nb=nb, ps=ps:
                                 e.matmul(ps[:], wo[:, k, m * 128:(m + 1) * 128],
                                          AO[:, k, nb * 512:(nb + 1) * 512],
                                          start=(k == 0), stop=(k == 7), skip_group_check=True),
                             wait=([("p1v", max(attn_v, p.c("p1v"))), ("p1g", wo_ld)]
                                   if k == 0 else []),
                             inc=(["p1t"] if k == 7 else []))
                    tmm = p.c("p1t")
                    p.op("vector",
                         lambda e, m=m, par=m % 2, ps=ps:
                             e.scalar_tensor_tensor(sbf[:, m, :], ps[:], bos[:, m:m + 1],
                                                    x0f[par][:], ALU.add, ALU.add),
                         wait=[("p1s", lx), ("p1t", tmm)], inc=["p1v"])
                    svm = p.c("p1v")
                    p.op("scalar",
                         lambda e, m=m: e.activation(sqbf[:], sbf[:, m, :], AF.Square),
                         wait=[("p1v", svm), ("p1t", sq_prev[0])], inc=["p1a"])
                    sqm = p.c("p1a")
                    p.op("tensor",
                         lambda e, m=m: e.matmul(psD[:, 0:512], g["ones128"][:], sbf[:, m, :],
                                                 start=(m == 0), stop=(m == 7),
                                                 skip_group_check=True),
                         wait=[("p1v", svm)] + ([("p1v", p.c("p1v"))] if m == 0 else []),
                         inc=(["p1t"] if m == 7 else []))
                    p.op("tensor",
                         lambda e, m=m: e.matmul(psD[:, 512:1024], g["ones128"][:], sqbf[:],
                                                 start=(m == 0), stop=(m == 7),
                                                 skip_group_check=True),
                         wait=[("p1a", sqm)], inc=["p1t"])
                    sq_prev[0] = p.c("p1t")
                tstat = p.c("p1t")
                p.op("vector", lambda e: e.tensor_scalar(Asc[:], psD[:, 0:512], 1.0 / D, None,
                                                         ALU.mult),
                     wait=[("p1t", tstat)], inc=["p1v"])
                p.op("vector", lambda e: e.tensor_scalar(tmp[:], psD[:, 512:1024], 1.0 / D, None,
                                                         ALU.mult), inc=["p1v"])
                p.op("vector", lambda e: e.tensor_tensor(Bsc[:], Asc[:], Asc[:], ALU.mult),
                     inc=["p1v"], ss="p1v")
                p.op("vector", lambda e: e.tensor_sub(tmp[:], tmp[:], Bsc[:]), inc=["p1v"], ss="p1v")
                p.op("vector", lambda e: e.tensor_scalar(tmp[:], tmp[:], EPS, None, ALU.add),
                     inc=["p1v"], ss="p1v")
                p.op("vector", lambda e: e.reciprocal(tmp[:], tmp[:]), inc=["p1v"], ss="p1v")
                md = p.c("p1v")
                p.op("scalar", lambda e: e.activation(tmp[:], tmp[:], AF.Sqrt),
                     wait=[("p1v", md)], inc=["p1a"], ss="p1a")
                rsd = p.c("p1a")
                p.op("vector", lambda e: e.scalar_tensor_tensor(Bsc[:], Asc[:], -1.0, tmp[:],
                                                                ALU.mult, ALU.mult),
                     wait=[("p1a", rsd)], inc=["p1v"], ss="p1v")
                for m in range(8):
                    buf = x1f[m % 2]
                    stg = [("p1s", p.c("p1s"))] if (nb or m >= 2) else []
                    p.op("vector",
                         lambda e, m=m, buf=buf: e.tensor_tensor(buf[:], sbf[:, m, :], tmp[:],
                                                                 ALU.mult),
                         wait=stg, inc=["p1v"], ss="p1v")
                    p.op("vector", lambda e, buf=buf: e.tensor_add(buf[:], buf[:], Bsc[:]),
                         inc=["p1v"], ss="p1v")
                    vd = p.c("p1v")
                    p.op("sync",
                         lambda e, m=m, jc=jc, buf=buf:
                             e.dma_start(out=io["x1"][m * 128:(m + 1) * 128, jc], in_=buf[:]),
                         wait=[("p1v", vd)], inc=["p1s"], dma=True)
        p.op("sync", lambda e: None, wait=[("p1s", p.c("p1s"))])
    p.barrier()


def emit_p3(p, io, g, NT=None):
    emit_p3a(p, io, g, NT=NT)
    emit_p3b(p, io, g, NT=NT)


def emit_p3a(p, io, g, NT=None):
    nc = p.nc
    NB = (T // 512) if NT is None else NT
    pre = "p3"
    with contextlib.ExitStack() as ctx:
        sb = lambda n, sh, dt: ctx.enter_context(nc.sbuf_tensor(pre + n, sh, dt))
        psA = ctx.enter_context(nc.psum_tensor(pre + "psA", [128, 512], F32))
        psB = ctx.enter_context(nc.psum_tensor(pre + "psB", [128, 512], F32))
        psU = [ctx.enter_context(nc.psum_tensor(pre + f"psU{i}", [128, 1024], BF16))
               for i in range(2)]

        x2b = sb("x2b", [128, 8, 512], BF16)
        qkfm = sb("qkfm", [128, 512], BF16)
        qt = sb("qt", [128, 4, 4096], BF16)
        kt = sb("kt", [128, 4, 4096], BF16)
        qr = sb("qr", [128, 4096], BF16)
        qi = sb("qi", [128, 4096], BF16)
        kr = sb("kr", [128, 4096], BF16)
        ki = sb("ki", [128, 4096], BF16)
        crep = sb("crep", [128, 4096], BF16)
        srep = sb("srep", [128, 4096], BF16)
        zpair = [sb(f"zp{i}", [128, 1024], BF16) for i in range(2)]
        zipair = [sb(f"zi{i}", [128, 1024], BF16) for i in range(2)]
        sr = sb("sr", [128, 16], F32)
        smx = sb("smx", [128, 4], F32)
        ssum = sb("ssum", [128, 4], F32)
        attn = [sb(f"attn{i}", [128, 16], F32) for i in range(2)]
        p3dbg = sb("p3dbg", [128, 112], F32) if "p3dbg" in io else None
        wstrip = [sb(f"ws{i}", [128, 8, 512], BF16) for i in range(2)]
        bqs = sb("bqs", [128, 32], F32)
        bks = sb("bks", [128, 32], F32)
        phc = sb("phc", [1, 512], F32)
        trow = [sb(f"trow{i}", [1, 512], F32) for i in range(2)]
        csrow = sb("csrow", [1, 1024], BF16)

        for name, tile in [("l1_bq", bqs), ("l1_bk", bks)]:
            p.op("sync", lambda e, t=tile, a=io[name]: e.dma_start(out=t[:], in_=a[:]),
                 inc=["p3s"], dma=True)
        M_RND = 12582912.0
        for c8 in range(8):
            gd = [("p3v", p.c("p3v"))] if c8 > 0 else []
            p.op("sync", lambda e, c8=c8: e.dma_start(out=phc[:],
                                                      in_=io["l1_ph"][:, c8 * 512:(c8 + 1) * 512]),
                 wait=gd, inc=["p3s"], dma=True)
            pld = p.c("p3s")
            p.op("vector", lambda e: e.tensor_scalar(trow[0][:], phc[:], 1.0 / (2 * np.pi), M_RND,
                                                     ALU.mult, ALU.add),
                 wait=[("p3s", pld), ("p3a", p.c("p3a"))], inc=["p3v"], ss="p3v")
            p.op("vector", lambda e: e.tensor_scalar(trow[0][:], trow[0][:], M_RND, None,
                                                     ALU.subtract), inc=["p3v"], ss="p3v")
            p.op("vector", lambda e: e.scalar_tensor_tensor(trow[1][:], trow[0][:], -2 * np.pi,
                                                            phc[:], ALU.mult, ALU.add),
                 inc=["p3v"], ss="p3v")
            svd = p.c("p3v")
            p.op("scalar", lambda e: e.activation(csrow[:, 512:1024], trow[1][:], AF.Sin),
                 wait=[("p3v", svd)], inc=["p3a"], ss="p3a")
            p.op("vector", lambda e: e.tensor_scalar(trow[1][:], phc[:], -1.0 / (2 * np.pi), 0.25,
                                                     ALU.mult, ALU.add),
                 wait=[("p3a", p.c("p3a"))], inc=["p3v"], ss="p3v")
            p.op("vector", lambda e: e.tensor_scalar(trow[0][:], trow[1][:], M_RND, None,
                                                     ALU.add), inc=["p3v"], ss="p3v")
            p.op("vector", lambda e: e.tensor_scalar(trow[0][:], trow[0][:], M_RND, None,
                                                     ALU.subtract), inc=["p3v"], ss="p3v")
            p.op("vector", lambda e: e.tensor_sub(trow[1][:], trow[1][:], trow[0][:]),
                 inc=["p3v"], ss="p3v")
            p.op("vector", lambda e: e.tensor_scalar(trow[1][:], trow[1][:], 2 * np.pi, None,
                                                     ALU.mult), inc=["p3v"], ss="p3v")
            cvd = p.c("p3v")
            p.op("scalar", lambda e: e.activation(csrow[:, 0:512], trow[1][:], AF.Sin),
                 wait=[("p3v", cvd)], inc=["p3a"], ss="p3a")
            rd = p.c("p3a")
            for (off, rep) in [(0, crep), (512, srep)]:
                ps = psA if off == 0 else psB
                p.op("tensor",
                     lambda e, off=off, ps=ps:
                         e.matmul(ps[:], g["ones1"][0:1, 0:128], csrow[:, off:off + 512],
                                  start=True, stop=True, skip_group_check=True),
                     wait=[("p3a", rd), ("p3v", p.c("p3v"))], inc=["p3t"])
                tmm = p.c("p3t")
                p.op("vector",
                     lambda e, rep=rep, c8=c8, ps=ps:
                         e.tensor_copy(rep[:, c8 * 512:(c8 + 1) * 512], ps[:]),
                     wait=[("p3t", tmm)], inc=["p3v"])
        rep_done = p.c("p3v")

        strip_free = [0, 0]
        for j in range(NB):
            jc = slice(j * 512, (j + 1) * 512)
            xguard = [("p3t", p.c("p3t"))] if j > 0 else []
            p.op("gpsimd", lambda e, jc=jc: e.dma_start(out=x2b[:], in_=_chunked(io["x2"])[:, :, jc]),
                 wait=xguard, inc=["p3g"], dma=True)
            xld = p.c("p3g")
            for (wname, bias, dst) in [("l1_wq", bqs, qt), ("l1_wk", bks, kt)]:
                for mg in range(8):
                    par = mg % 2
                    p.op("gpsimd",
                         lambda e, wname=wname, mg=mg, par=par:
                             e.dma_start(out=wstrip[par][:],
                                         in_=_chunked(io[wname])[:, :, mg * 512:(mg + 1) * 512]),
                         wait=[("p3t", strip_free[par])], inc=["p3g"], dma=True)
                    wld = p.c("p3g")
                    for m in range(4):
                        mo = mg * 4 + m
                        ps = psA if mo % 2 == 0 else psB
                        for k in range(8):
                            p.op("tensor",
                                 lambda e, k=k, par=par, m=m, ps=ps:
                                     e.matmul(ps[:], wstrip[par][:, k, m * 128:(m + 1) * 128],
                                              x2b[:, k, :], start=(k == 0), stop=(k == 7),
                                              skip_group_check=True),
                                 wait=([("p3g", wld), ("p3g", xld), ("p3a", p.c("p3a"))]
                                       if k == 0 else []),
                                 inc=(["p3t"] if k == 7 else []))
                        p.op("scalar",
                             lambda e, mo=mo, ps=ps, bias=bias:
                                 e.activation(qkfm[:], ps[:], AF.Identity,
                                              bias=bias[:, mo:mo + 1]),
                             wait=[("p3t", p.c("p3t"))], inc=["p3a"])
                        fmd = p.c("p3a")
                        for ts in range(4):
                            pu = psU[ts % 2]
                            p.op("tensor",
                                 lambda e, ts=ts, pu=pu:
                                     e.transpose(pu[:, 0:128], qkfm[:, ts * 128:(ts + 1) * 128],
                                                 g["identb"][:]),
                                 wait=[("p3a", fmd), ("p3v", p.c("p3v"))], inc=["p3t"])
                            trd = p.c("p3t")
                            p.op("vector",
                                 lambda e, ts=ts, mo=mo, dst=dst, pu=pu:
                                     e.tensor_copy(dst[:, ts, mo * 128:(mo + 1) * 128],
                                                   pu[:, 0:128]),
                                 wait=[("p3t", trd)], inc=["p3v"])
                    strip_free[par] = p.c("p3t")
            qkv_done_v = p.c("p3v")

            for ts in range(4):
                t0 = j * 4 + ts
                for (dstt, srct, rep) in [(qr, qt, crep), (qi, qt, srep),
                                          (kr, kt, crep), (ki, kt, srep)]:
                    p.op("vector",
                         lambda e, dstt=dstt, srct=srct, rep=rep, ts=ts:
                             e.tensor_mul(dstt[:], srct[:, ts, :], rep[:]),
                         wait=([("p3v", rep_done)] if j == 0 and ts == 0 else []),
                         inc=["p3v"], ss=("p3v" if ts == 0 else None))
                mul_done = p.c("p3v")
                for g_ in range(4):
                    for h_ in range(4):
                        par2 = (g_ * 4 + h_) % 2
                        zz = zpair[par2]
                        zi = zipair[par2]
                        aguard = [("p3a", p.c("p3a"))] if (g_ * 4 + h_) >= 2 else []
                        p.op("vector",
                             lambda e, g_=g_, h_=h_, zz=zz:
                                 e.tensor_mul(zz[:], qr[:, h_ * 1024:(h_ + 1) * 1024],
                                              kr[:, g_ * 1024:(g_ + 1) * 1024]),
                             wait=aguard, inc=["p3v"],
                             ss=("p3v" if g_ == 0 and h_ == 0 else None))
                        p.op("vector",
                             lambda e, g_=g_, h_=h_, zi=zi:
                                 e.tensor_mul(zi[:], qi[:, h_ * 1024:(h_ + 1) * 1024],
                                              ki[:, g_ * 1024:(g_ + 1) * 1024]),
                             inc=["p3v"])
                        p.op("vector",
                             lambda e, zz=zz, zi=zi: e.tensor_add(zz[:], zz[:], zi[:]),
                             inc=["p3v"], ss="p3v")
                        zd = p.c("p3v")
                        p.op("scalar",
                             lambda e, g_=g_, h_=h_, zz=zz:
                                 e.activation(zz[:], zz[:], AF.Identity, scale=1.0 / 32,
                                              accum_out=sr[:, h_ * 4 + g_:h_ * 4 + g_ + 1]),
                             wait=[("p3v", zd)], inc=["p3a"],
                             ss=("p3a" if (g_ * 4 + h_) < 2 else None))
                sdots = p.c("p3a")
                if p3dbg is not None and j == 0 and ts == 0:
                    p.op("vector", lambda e: e.tensor_copy(p3dbg[:, 0:16], sr[:]),
                         wait=[("p3a", sdots)], inc=["p3v"])
                    p.op("vector", lambda e: e.tensor_copy(p3dbg[:, 16:32], qt[:, 0, 0:16]), inc=["p3v"])
                    p.op("vector", lambda e: e.tensor_copy(p3dbg[:, 32:48], kt[:, 0, 0:16]), inc=["p3v"])
                    p.op("vector", lambda e: e.tensor_copy(p3dbg[:, 48:64], crep[:, 0:16]), inc=["p3v"])
                    p.op("vector", lambda e: e.tensor_copy(p3dbg[:, 64:80], srep[:, 0:16]), inc=["p3v"])
                    p.op("vector", lambda e: e.tensor_copy(p3dbg[:, 80:96], qr[:, 0:16]), inc=["p3v"])
                    p.op("vector", lambda e: e.tensor_copy(p3dbg[:, 96:112], kr[:, 0:16]), inc=["p3v"])
                sr3 = sr[:].rearrange("p (h g2) -> p h g2", h=4)
                p.op("vector", lambda e, sr3=sr3: e.reduce_max(smx[:], sr3, axis=AX.X),
                     wait=[("p3a", sdots)], inc=["p3v"], ss="p3v")
                p.op("vector",
                     lambda e, sr3=sr3: e.tensor_tensor(
                         sr3, sr3, smx[:].unsqueeze(2).broadcast_to([128, 4, 4]), ALU.subtract),
                     inc=["p3v"], ss="p3v")
                evd = p.c("p3v")
                p.op("scalar", lambda e: e.activation(sr[:], sr[:], AF.Exp),
                     wait=[("p3v", evd)], inc=["p3a"], ss="p3a")
                exd = p.c("p3a")
                p.op("vector", lambda e, sr3=sr3: e.reduce_sum(ssum[:], sr3, axis=AX.X),
                     wait=[("p3a", exd)], inc=["p3v"], ss="p3v")
                p.op("vector", lambda e: e.reciprocal(ssum[:], ssum[:]), inc=["p3v"], ss="p3v")
                ab = attn[ts % 2]
                p.op("vector",
                     lambda e, sr3=sr3, ab=ab: e.tensor_tensor(
                         ab[:].rearrange("p (h g2) -> p h g2", h=4), sr3,
                         ssum[:].unsqueeze(2).broadcast_to([128, 4, 4]), ALU.mult),
                     wait=[("p3s", p.c("p3s"))], inc=["p3v"], ss="p3v")
                atd = p.c("p3v")
                p.op("sync",
                     lambda e, t0=t0, ab=ab:
                         e.dma_start(out=io["p3attn"][t0 * 128:(t0 + 1) * 128, :], in_=ab[:]),
                     wait=[("p3v", atd)], inc=["p3s"], dma=True)
        if p3dbg is not None:
            p.op("sync", lambda e: e.dma_start(out=io["p3dbg"][:], in_=p3dbg[:]),
                 wait=[("p3v", p.c("p3v"))], inc=["p3s"], dma=True)
        p.op("sync", lambda e: None, wait=[("p3s", p.c("p3s"))])
    p.barrier()


def emit_p3b(p, io, g, NT=None):
    nc = p.nc
    NB = (T // 512) if NT is None else NT
    pre = "q3"
    with contextlib.ExitStack() as ctx:
        sb = lambda n, sh, dt: ctx.enter_context(nc.sbuf_tensor(pre + n, sh, dt))
        psA = ctx.enter_context(nc.psum_tensor(pre + "psA", [128, 512], F32))
        psB = ctx.enter_context(nc.psum_tensor(pre + "psB", [128, 512], F32))
        psC = ctx.enter_context(nc.psum_tensor(pre + "psC", [128, 512], F32))
        psD = ctx.enter_context(nc.psum_tensor(pre + "psD", [128, 1536], F32))
        psU = ctx.enter_context(nc.psum_tensor(pre + "psU", [128, 1024], BF16))

        x2b = sb("x2b", [128, 8, 512], BF16)
        vfm = sb("vfm", [128, 32, 512], BF16)
        attn = [sb(f"attn{i}", [128, 16], BF16) for i in range(2)]
        attnT = sb("attnT", [16, 512], BF16)
        sel = sb("sel", [16, 2048], BF16)
        Atile = sb("Atile", [128, 16, 512], BF16)
        mixed = sb("mixed", [128, 32, 512], BF16)
        z4 = [sb(f"z4{i}", [128, 512], BF16) for i in range(4)]
        wstrip = [sb(f"ws{i}", [128, 8, 512], BF16) for i in range(2)]
        wostrip = [sb(f"wo{i}", [128, 32, 128], BF16) for i in range(2)]
        bvs = sb("bvs", [128, 32], F32)
        bos = sb("bos", [128, 8], F32)
        sbf = sb("sbf", [128, 8, 512], BF16)
        sqbf = sb("sqbf", [128, 512], BF16)
        Asc = sb("Asc", [128, 512], F32)
        Bsc = sb("Bsc", [128, 512], F32)
        tmp = sb("tmp", [128, 512], F32)
        x2f = [sb(f"x2f{i}", [128, 512], F32) for i in range(2)]
        x3f = [sb(f"x3f{i}", [128, 512], F32) for i in range(2)]
        x3tt = [sb(f"x3tt{i}", [128, 128], F32) for i in range(2)]

        for name, tile in [("l1_bv", bvs), ("l1_bo", bos)]:
            p.op("sync", lambda e, t=tile, a=io[name]: e.dma_start(out=t[:], in_=a[:]),
                 inc=["q3s"], dma=True)
        setup_ld = p.c("q3s")
        p.op("gpsimd", lambda e: e.iota(sel[:].rearrange("p (a b) -> p a b", a=16),
                                        [[-1, 16], [0, 128]], channel_multiplier=1,
                                        allow_small_or_imprecise_dtypes=True), inc=["q3g1"])
        iod = p.c("q3g1")
        p.op("vector", lambda e: e.tensor_single_scalar(sel[:], sel[:], 0.0, ALU.is_equal),
             wait=[("q3g1", iod)], inc=["q3v"])
        sel_done = p.c("q3v")

        strip_free = [0, 0]
        wo_free = [0, 0]
        sq_prev3 = [0]
        for j in range(NB):
            jc = slice(j * 512, (j + 1) * 512)
            xguard = [("q3t", p.c("q3t"))] if j > 0 else []
            p.op("gpsimd", lambda e, jc=jc: e.dma_start(out=x2b[:], in_=_chunked(io["x2"])[:, :, jc]),
                 wait=xguard, inc=["q3g"], dma=True)
            xld = p.c("q3g")
            for mg in range(8):
                par = mg % 2
                p.op("gpsimd",
                     lambda e, mg=mg, par=par:
                         e.dma_start(out=wstrip[par][:],
                                     in_=_chunked(io["l1_wv"])[:, :, mg * 512:(mg + 1) * 512]),
                     wait=[("q3t", strip_free[par])], inc=["q3g"], dma=True)
                wld = p.c("q3g")
                for m in range(4):
                    mo = mg * 4 + m
                    ps = psA if mo % 2 == 0 else psB
                    for k in range(8):
                        p.op("tensor",
                             lambda e, k=k, par=par, m=m, ps=ps:
                                 e.matmul(ps[:], wstrip[par][:, k, m * 128:(m + 1) * 128],
                                          x2b[:, k, :], start=(k == 0), stop=(k == 7),
                                          skip_group_check=True),
                             wait=([("q3g", wld), ("q3g", xld), ("q3a", p.c("q3a"))]
                                   if k == 0 else []),
                             inc=(["q3t"] if k == 7 else []))
                    tmm = p.c("q3t")
                    p.op("scalar",
                         lambda e, mo=mo, ps=ps:
                             e.activation(vfm[:, mo, :], ps[:], AF.Identity,
                                          bias=bvs[:, mo:mo + 1]),
                         wait=[("p3t" if False else "q3t", tmm),
                               ("q3s", setup_ld)], inc=["q3a"])
                strip_free[par] = p.c("q3t")
            v_done = p.c("q3a")

            for ts in range(4):
                t0 = j * 4 + ts
                ab = attn[ts % 2]
                p.op("gpsimd",
                     lambda e, t0=t0, ab=ab:
                         e.dma_start(out=ab[:], in_=io["p3attn"][t0 * 128:(t0 + 1) * 128, :]),
                     wait=[("q3t", p.c("q3t"))] if ts >= 2 else [], inc=["q3g"], dma=True)
                ald = p.c("q3g")
                p.op("tensor", lambda e, ab=ab: e.transpose(psU[0:16, 0:128], ab[:], g["identb"][:]),
                     wait=[("q3g", ald), ("q3v", p.c("q3v"))], inc=["q3t"])
                atr = p.c("q3t")
                p.op("vector",
                     lambda e, ts=ts: e.tensor_copy(attnT[:, ts * 128:(ts + 1) * 128],
                                                    psU[0:16, 0:128]),
                     wait=[("q3t", atr)], inc=["q3v"])
            attnT_done = p.c("q3v")

            for hg in range(16):
                p.op("tensor",
                     lambda e, hg=hg: e.matmul(psC[:], sel[0:16, hg * 128:(hg + 1) * 128],
                                               attnT[:], start=True, stop=True,
                                               skip_group_check=True),
                     wait=[("q3v", p.c("q3v"))], inc=["q3t"])
                amm = p.c("q3t")
                p.op("vector", lambda e, hg=hg: e.tensor_copy(Atile[:, hg, :], psC[:]),
                     wait=[("q3t", amm)], inc=["q3v"])
            a_done = p.c("q3v")
            for mi in range(32):
                h_, c_ = mi // 8, mi % 8
                for g_ in range(4):
                    p.op("vector",
                         lambda e, h_=h_, c_=c_, g_=g_:
                             e.tensor_mul(z4[g_][:], Atile[:, h_ * 4 + g_, :],
                                          vfm[:, g_ * 8 + c_, :]),
                         wait=([("q3a", v_done)] if mi == 0 and g_ == 0 else []),
                         inc=["q3v"], ss=("q3v" if g_ == 0 else None))
                p.op("vector", lambda e: e.tensor_add(z4[0][:], z4[0][:], z4[1][:]),
                     inc=["q3v"], ss="q3v")
                p.op("vector", lambda e: e.tensor_add(z4[2][:], z4[2][:], z4[3][:]),
                     inc=["q3v"], ss="q3v")
                p.op("vector", lambda e, mi=mi: e.tensor_add(
                         mixed[:, mi, :].rearrange("p (g i4) -> p i4 g", g=4),
                         z4[0][:].rearrange("p (i4 g) -> p i4 g", g=4),
                         z4[2][:].rearrange("p (i4 g) -> p i4 g", g=4)),
                     inc=["q3v"], ss="q3v")
            mixed_done = p.c("q3v")

            seq, jj = j // 4, j % 4
            colv = lambda ap: ap.rearrange("d (sq ih b i4) -> d sq ih b i4", sq=4, ih=4, b=4)
            for m in range(8):
                par = m % 2
                p.op("gpsimd",
                     lambda e, m=m, par=par:
                         e.dma_start(out=wostrip[par][:],
                                     in_=_chunked(io["l1_wo"])[:, :, m * 128:(m + 1) * 128]),
                     wait=[("q3t", wo_free[par])], inc=["q3g"], dma=True)
                wld = p.c("q3g")
                p.op("sync",
                     lambda e, m=m, par=par, seq=seq, jj=jj:
                         e.dma_start(out=x2f[par][:].rearrange("p (ih i4) -> p ih i4", ih=4),
                                     in_=colv(io["x2"])[m * 128:(m + 1) * 128, seq, :, jj, :]),
                     wait=[("q3v", p.c("q3v"))] if m >= 2 else [], inc=["q3s"], dma=True)
                lx = p.c("q3s")
                ps = psA if m % 2 == 0 else psB
                for ih in range(4):
                    for k in range(32):
                        g_, c_ = k // 8, k % 8
                        p.op("tensor",
                             lambda e, k=k, par=par, ps=ps, ih=ih, g_=g_, c_=c_:
                                 e.matmul(ps[:, ih * 128:(ih + 1) * 128],
                                          wostrip[par][:, k, :],
                                          mixed[:, ih * 8 + c_, g_ * 128:(g_ + 1) * 128],
                                          start=(k == 0), stop=(k == 31),
                                          skip_group_check=True),
                             wait=([("q3g", wld), ("q3v", p.c("q3v"))]
                                   if ih == 0 and k == 0 else []),
                             inc=(["q3t"] if ih == 3 and k == 31 else []))
                tmm = p.c("q3t")
                wo_free[par] = tmm
                p.op("vector",
                     lambda e, m=m, par=par, ps=ps:
                         e.scalar_tensor_tensor(sbf[:, m, :], ps[:], bos[:, m:m + 1],
                                                x2f[par][:], ALU.add, ALU.add),
                     wait=[("q3s", lx), ("q3t", tmm)], inc=["q3v"])
                svm = p.c("q3v")
                p.op("scalar",
                     lambda e, m=m: e.activation(sqbf[:], sbf[:, m, :], AF.Square),
                     wait=[("q3v", svm), ("q3t", sq_prev3[0])], inc=["q3a"])
                sqm = p.c("q3a")
                p.op("tensor",
                     lambda e, m=m: e.matmul(psD[:, 0:512], g["ones128"][:], sbf[:, m, :],
                                             start=(m == 0), stop=(m == 7), skip_group_check=True),
                     wait=[("q3v", svm)] + ([("q3v", p.c("q3v"))] if m == 0 else []),
                     inc=(["q3t"] if m == 7 else []))
                p.op("tensor",
                     lambda e, m=m: e.matmul(psD[:, 512:1024], g["ones128"][:], sqbf[:],
                                             start=(m == 0), stop=(m == 7), skip_group_check=True),
                     wait=[("q3a", sqm)], inc=["q3t"])
                sq_prev3[0] = p.c("q3t")
            tstat = p.c("q3t")
            p.op("vector", lambda e: e.tensor_scalar(Asc[:], psD[:, 0:512], 1.0 / D, None, ALU.mult),
                 wait=[("q3t", tstat)], inc=["q3v"])
            p.op("vector", lambda e: e.tensor_scalar(tmp[:], psD[:, 512:1024], 1.0 / D, None, ALU.mult),
                 inc=["q3v"])
            p.op("vector", lambda e: e.tensor_tensor(Bsc[:], Asc[:], Asc[:], ALU.mult),
                 inc=["q3v"], ss="q3v")
            p.op("vector", lambda e: e.tensor_sub(tmp[:], tmp[:], Bsc[:]), inc=["q3v"], ss="q3v")
            p.op("vector", lambda e: e.tensor_scalar(tmp[:], tmp[:], EPS, None, ALU.add),
                 inc=["q3v"], ss="q3v")
            p.op("vector", lambda e: e.reciprocal(tmp[:], tmp[:]), inc=["q3v"], ss="q3v")
            md = p.c("q3v")
            p.op("scalar", lambda e: e.activation(tmp[:], tmp[:], AF.Sqrt),
                 wait=[("q3v", md)], inc=["q3a"], ss="q3a")
            rsd = p.c("q3a")
            p.op("vector", lambda e: e.scalar_tensor_tensor(Bsc[:], Asc[:], -1.0, tmp[:],
                                                            ALU.mult, ALU.mult),
                 wait=[("q3a", rsd)], inc=["q3v"], ss="q3v")
            for m in range(8):
                buf = x3f[m % 2]
                stg = [("q3s", p.c("q3s"))] if m >= 2 else []
                p.op("vector",
                     lambda e, m=m, buf=buf: e.tensor_tensor(buf[:], sbf[:, m, :], tmp[:], ALU.mult),
                     wait=stg, inc=["q3v"], ss="q3v")
                p.op("vector", lambda e, m=m, buf=buf: e.tensor_add(buf[:], buf[:], Bsc[:]),
                     inc=["q3v"], ss="q3v")
                vd = p.c("q3v")
                p.op("sync",
                     lambda e, m=m, buf=buf, seq=seq, jj=jj:
                         e.dma_start(out=colv(io["x3"])[m * 128:(m + 1) * 128, seq, :, jj, :],
                                     in_=buf[:].rearrange("p (ih i4) -> p ih i4", ih=4)),
                     wait=[("q3v", vd)], inc=["q3s"], dma=True)
                for ts in range(4):
                    p.op("tensor",
                         lambda e, ts=ts, buf=buf:
                             e.transpose(psD[:, 1024:1152],
                                         buf[:, ts * 128:(ts + 1) * 128], g["identf32"][:]),
                         wait=[("q3v", p.c("q3v"))], inc=["q3t"])
                    trd = p.c("q3t")
                    tb = x3tt[ts % 2]
                    p.op("vector", lambda e, tb=tb: e.tensor_copy(tb[:], psD[:, 1024:1152]),
                         wait=[("q3t", trd), ("q3s", p.c("q3s"))], inc=["q3v"])
                    cvd2 = p.c("q3v")
                    row0 = seq * 2048 + ts * 512 + jj * 128
                    p.op("sync",
                         lambda e, m=m, row0=row0, tb=tb:
                             e.dma_start(out=io["x3t"][row0:row0 + 128,
                                                       m * 128:(m + 1) * 128], in_=tb[:]),
                         wait=[("q3v", cvd2)], inc=["q3s"], dma=True)
        p.op("sync", lambda e: None, wait=[("q3s", p.c("q3s"))])
    p.barrier()


def emit_p4(p, io, g, NT=None, skip_ln=False, coef_out=None):
    nc = p.nc
    NTILE = (T // 128) if NT is None else NT * 4
    NB = NTILE // 4
    pre = "p4"
    with contextlib.ExitStack() as ctx:
        sb = lambda n, sh, dt: ctx.enter_context(nc.sbuf_tensor(pre + n, sh, dt))
        psD = ctx.enter_context(nc.psum_tensor(pre + "psD", [128, 2048], F32))
        psA = ctx.enter_context(nc.psum_tensor(pre + "psA", [128, 512], F32))
        psB = ctx.enter_context(nc.psum_tensor(pre + "psB", [128, 512], F32))
        psC = ctx.enter_context(nc.psum_tensor(pre + "psC", [128, 512], F32))
        coef = sb("coef", [128, NTILE * 8], F32)
        gdbg = sb("gdbg", [128, 32], F32) if coef_out else None

        with contextlib.ExitStack() as gtx:
            sbg = lambda n, sh, dt: gtx.enter_context(nc.sbuf_tensor(pre + n, sh, dt))
            x3g = sbg("x3g", [128, 8, 512], F32)
            wgf = sbg("wgf", [128, 8, 8], F32)
            bgf = sbg("bgf", [1, 8], F32)
            lg = sbg("lg", [128, 8], F32)
            mx = sbg("mx", [128, 1], F32)
            negmx = sbg("negmx", [128, 1], F32)
            ex = sbg("ex", [128, 8], F32)
            ssum = sbg("ssum", [128, 1], F32)
            p.op("sync", lambda e: e.dma_start(out=wgf[:], in_=_chunked(io["wg"])[:]),
                 inc=["p4s"], dma=True)
            p.op("sync", lambda e: e.dma_start(out=bgf[:], in_=io["bg"][:]),
                 inc=["p4s"], dma=True)
            wld = p.c("p4s")
            for j in range(NB):
                jc = slice(j * 512, (j + 1) * 512)
                guard = [("p4t", p.c("p4t"))] if j > 0 else []
                p.op("sync", lambda e, jc=jc: e.dma_start(out=x3g[:], in_=_chunked(io["x3"])[:, :, jc]),
                     wait=guard, inc=["p4s"], dma=True)
                xld = p.c("p4s")
                for ts in range(4):
                    t0 = j * 4 + ts
                    tsl = slice(ts * 128, (ts + 1) * 128)
                    for k in range(8):
                        p.op("tensor",
                             lambda e, k=k, tsl=tsl: e.matmul(psC[:, 0:8], x3g[:, k, tsl],
                                                              wgf[:, k, :], start=(k == 0),
                                                              stop=False, skip_group_check=True),
                             wait=([("p4s", xld), ("p4v", p.c("p4v"))] if k == 0 else []),
                             inc=[])
                    p.op("tensor",
                         lambda e: e.matmul(psC[:, 0:8], g["onesf1"][:], bgf[:],
                                            start=False, stop=True, skip_group_check=True),
                         inc=["p4t"])
                    gm = p.c("p4t")
                    p.op("vector", lambda e: e.tensor_copy(lg[:], psC[:, 0:8]),
                         wait=[("p4t", gm)], inc=["p4v"])
                    p.op("vector", lambda e: e.reduce_max(mx[:], lg[:], axis=AX.X), inc=["p4v"],
                         ss="p4v")
                    p.op("vector", lambda e: e.tensor_scalar(negmx[:], mx[:], -1.0, None, ALU.mult),
                         inc=["p4v"], ss="p4v")
                    vd = p.c("p4v")
                    p.op("scalar", lambda e: e.activation(ex[:], lg[:], AF.Exp, bias=negmx[:],
                                                          accum_out=ssum[:]),
                         wait=[("p4v", vd)], inc=["p4a"])
                    ad = p.c("p4a")
                    p.op("vector", lambda e: e.reciprocal(ssum[:], ssum[:]),
                         wait=[("p4a", ad)], inc=["p4v"], ss="p4v")
                    if gdbg is not None and t0 == 0:
                        p.op("vector", lambda e: e.tensor_copy(gdbg[:, 0:8], lg[:]), inc=["p4v"])
                        p.op("vector", lambda e: e.tensor_copy(gdbg[:, 8:9], mx[:]), inc=["p4v"])
                        p.op("vector", lambda e: e.tensor_copy(gdbg[:, 9:10], negmx[:]), inc=["p4v"])
                        p.op("vector", lambda e: e.tensor_copy(gdbg[:, 10:11], ssum[:]), inc=["p4v"])
                        p.op("vector", lambda e: e.tensor_copy(gdbg[:, 11:19], ex[:]), inc=["p4v"])
                    for ee in range(8):
                        p.op("vector",
                             lambda e, ee=ee, t0=t0:
                                 e.tensor_scalar(coef[:, t0 * 8 + ee:t0 * 8 + ee + 1],
                                                 lg[:, ee:ee + 1], mx[:], ssum[:],
                                                 ALU.is_equal, ALU.mult),
                             inc=["p4v"], ss=("p4v" if ee == 0 else None))
            gate_done_v = p.c("p4v")
            gate_done_t = p.c("p4t")

        for m in range(8):
            p.op("sync",
                 lambda e, m=m: e.dma_start(out=io["out"][m * (T // 8):(m + 1) * (T // 8), :],
                                            in_=io["x3t"][m * (T // 8):(m + 1) * (T // 8), :]),
                 inc=["p4s"], dma=True)
        init_done = p.c("p4s")

        w1e = sb("w1e", [128, 8, DH], BF16)
        w2e = sb("w2e", [128, 32, D], BF16)
        b1s = sb("b1s", [128, 32], F32)
        b2r = sb("b2r", [1, D], BF16)
        x3b = sb("x3b", [128, 8, 512], BF16)
        hbuf = sb("hbuf", [128, 32, 512], BF16)
        of32 = [sb(f"of{i}", [128, D], F32) for i in range(2)]
        for e_i in range(NEXP):
            g1 = [("p4t", p.c("p4t")), ("p4v", gate_done_v)] if e_i > 0 else [("p4v", gate_done_v)]
            for k in range(8):
                p.op("gpsimd",
                     lambda e, e_i=e_i, k=k: e.dma_start(out=w1e[:, k, :],
                                                         in_=_chunked(io["m_w1"][e_i])[:, k, :]),
                     wait=(g1 if k == 0 else []), inc=["p4g"], dma=True)
            for c4 in range(8):
                p.op("gpsimd",
                     lambda e, e_i=e_i, c4=c4: e.dma_start(
                         out=w2e[:, c4 * 4:(c4 + 1) * 4, :],
                         in_=_chunked(io["m_w2"][e_i])[:, c4 * 4:(c4 + 1) * 4, :]),
                     inc=["p4g"], dma=True)
            p.op("gpsimd", lambda e, e_i=e_i: e.dma_start(out=b1s[:], in_=io["m_b1"][e_i]),
                 inc=["p4g"], dma=True)
            p.op("gpsimd", lambda e, e_i=e_i: e.dma_start(out=b2r[:], in_=io["m_b2"][e_i]),
                 inc=["p4g"], dma=True)
            wld = p.c("p4g")
            for j in range(NB):
                jc = slice(j * 512, (j + 1) * 512)
                xguard = [("p4t", p.c("p4t"))] if (e_i > 0 or j > 0) else []
                p.op("gpsimd", lambda e, jc=jc: e.dma_start(out=x3b[:], in_=_chunked(io["x3"])[:, :, jc]),
                     wait=xguard, inc=["p4g"], dma=True)
                xld = p.c("p4g")
                for mh in range(32):
                    ps = psA if mh % 2 == 0 else psB
                    for k in range(8):
                        p.op("tensor",
                             lambda e, mh=mh, k=k, ps=ps:
                                 e.matmul(ps[:], w1e[:, k, mh * 128:(mh + 1) * 128], x3b[:, k, :],
                                          start=(k == 0), stop=(k == 7), skip_group_check=True),
                             wait=([("p4g", xld), ("p4a", p.c("p4a") - (1 if mh >= 2 else 0))]
                                   if k == 0 else []),
                             inc=(["p4t"] if k == 7 else []))
                    hm = p.c("p4t")
                    p.op("scalar",
                         lambda e, mh=mh, ps=ps: e.activation(hbuf[:, mh, :], ps[:], AF.Silu,
                                                              bias=b1s[:, mh:mh + 1]),
                         wait=[("p4t", hm)], inc=["p4a"])
                hdone = p.c("p4a")
                for ts in range(4):
                    t0 = j * 4 + ts
                    tsl = slice(ts * 128, (ts + 1) * 128)
                    pso = psD[:, 0:1024] if ts % 2 == 0 else psD[:, 1024:2048]
                    for nh in range(2):
                        for k in range(32):
                            p.op("tensor",
                                 lambda e, k=k, tsl=tsl, nh=nh, pso=pso:
                                     e.matmul(pso[:, nh * 512:(nh + 1) * 512],
                                              hbuf[:, k, tsl], w2e[:, k, nh * 512:(nh + 1) * 512],
                                              start=(k == 0), stop=False, skip_group_check=True),
                                 wait=([("p4a", hdone), ("p4v", p.c("p4v") - (1 if (e_i > 0 or j > 0 or ts >= 2) else 0))]
                                       if k == 0 and nh == 0 else []),
                                 inc=[])
                        p.op("tensor",
                             lambda e, tsl=tsl, nh=nh, pso=pso:
                                 e.matmul(pso[:, nh * 512:(nh + 1) * 512], g["ones1"][0:1, 0:128],
                                          b2r[:, nh * 512:(nh + 1) * 512],
                                          start=False, stop=True, skip_group_check=True),
                             inc=(["p4t"] if nh == 1 else []))
                    om = p.c("p4t")
                    ob = of32[ts % 2]
                    stguard = [("p4g", p.c("p4g"))] if (e_i > 0 or j > 0 or ts >= 2) else []
                    p.op("vector",
                         lambda e, t0=t0, e_i=e_i, pso=pso, ob=ob:
                             e.tensor_scalar(ob[:], pso[:], coef[:, t0 * 8 + e_i:t0 * 8 + e_i + 1],
                                             None, ALU.mult),
                         wait=[("p4t", om)] + stguard, inc=["p4v"])
                    ovd = p.c("p4v")
                    p.op("gpsimd",
                         lambda e, t0=t0, ob=ob:
                             e.dma_start(out=io["out"][t0 * 128:(t0 + 1) * 128, :], in_=ob[:],
                                         accum_op=ALU.add),
                         wait=[("p4v", ovd)] + ([("p4s", init_done)] if e_i == 0 and j == 0 else []),
                         inc=["p4g"], dma=True)
        acc_done = p.c("p4g")
        if coef_out is not None:
            p.op("sync", lambda e: e.dma_start(out=io[coef_out][:, 0:NTILE * 8], in_=coef[:]),
                 wait=[("p4v", p.c("p4v"))], inc=["p4s"], dma=True)
            p.op("sync", lambda e: e.dma_start(out=io["gdbgo"][:], in_=gdbg[:]),
                 inc=["p4s"], dma=True)
        if skip_ln:
            p.op("sync", lambda e: None, wait=[("p4g", acc_done), ("p4s", p.c("p4s"))])
            p.barrier()
            return

        yt = [sb(f"yt{i}", [128, D], F32) for i in range(2)]
        yq = sb("yq", [128, D], F32)
        s1 = sb("sA", [128, 1], F32)
        s2 = sb("sB", [128, 1], F32)
        s3 = sb("sC", [128, 1], F32)
        for t0 in range(NTILE):
            buf = yt[t0 % 2]
            lguard = [("p4s", p.c("p4s"))] if t0 >= 2 else [("p4g", acc_done)]
            p.op("sync", lambda e, t0=t0, buf=buf:
                     e.dma_start(out=buf[:], in_=io["out"][t0 * 128:(t0 + 1) * 128, :]),
                 wait=lguard, inc=["p4s"], dma=True)
            yld = p.c("p4s")
            p.op("vector", lambda e, buf=buf: e.reduce_sum(s1[:], buf[:], axis=AX.X),
                 wait=[("p4s", yld)], inc=["p4v"], ss="p4v")
            vd = p.c("p4v")
            p.op("scalar", lambda e, buf=buf: e.activation(yq[:], buf[:], AF.Square, accum_out=s2[:]),
                 wait=[("p4s", yld), ("p4v", p.c("p4v"))], inc=["p4a"])
            aq = p.c("p4a")
            p.op("vector", lambda e: e.tensor_scalar(s1[:], s1[:], 1.0 / D, None, ALU.mult),
                 inc=["p4v"], ss="p4v")
            p.op("vector", lambda e: e.tensor_scalar(s2[:], s2[:], 1.0 / D, None, ALU.mult),
                 wait=[("p4a", aq)], inc=["p4v"], ss="p4v")
            p.op("vector", lambda e: e.tensor_tensor(s3[:], s1[:], s1[:], ALU.mult), inc=["p4v"], ss="p4v")
            p.op("vector", lambda e: e.tensor_sub(s2[:], s2[:], s3[:]), inc=["p4v"], ss="p4v")
            p.op("vector", lambda e: e.tensor_scalar(s2[:], s2[:], EPS, None, ALU.add), inc=["p4v"], ss="p4v")
            p.op("vector", lambda e: e.reciprocal(s2[:], s2[:]), inc=["p4v"], ss="p4v")
            rv = p.c("p4v")
            p.op("scalar", lambda e: e.activation(s2[:], s2[:], AF.Sqrt),
                 wait=[("p4v", rv)], inc=["p4a"])
            rs = p.c("p4a")
            p.op("vector", lambda e, buf=buf: e.tensor_scalar(buf[:], buf[:], s1[:], s2[:],
                                                              ALU.subtract, ALU.mult),
                 wait=[("p4a", rs)], inc=["p4v"], ss="p4v")
            nd = p.c("p4v")
            p.op("sync", lambda e, t0=t0, buf=buf:
                     e.dma_start(out=io["out"][t0 * 128:(t0 + 1) * 128, :], in_=buf[:]),
                 wait=[("p4v", nd)], inc=["p4s"], dma=True)
        p.op("sync", lambda e: None, wait=[("p4s", p.c("p4s"))])
    p.barrier()


def emit_globals(p, g):
    nc = p.nc
    p.op("gpsimd", lambda e: e.iota(g["iotaf"][:], [[-1, 128]], channel_multiplier=1,
                                    allow_small_or_imprecise_dtypes=True), inc=["gg"])
    iot = p.c("gg")
    p.op("vector", lambda e: e.tensor_single_scalar(g["identb"][:], g["iotaf"][:], 0.0, ALU.is_equal),
         wait=[("gg", iot)], inc=["gv"])
    p.op("vector", lambda e: e.tensor_single_scalar(g["identf32"][:], g["iotaf"][:], 0.0, ALU.is_equal),
         inc=["gv"])
    p.op("vector", lambda e: e.memset(g["ones128"][:], 1.0), inc=["gv"])
    p.op("vector", lambda e: e.memset(g["ones1"][:], 1.0), inc=["gv"])
    p.op("vector", lambda e: e.memset(g["onesf1"][:], 1.0), inc=["gv"])
    idone = p.c("gv")
    p.op("vector", lambda e: e.memset(g["iotaf"][:], 0.0), inc=["gv"])
    zdone = p.c("gv")
    p.op("gpsimd", lambda e: e.affine_select(g["maskf"][:], g["iotaf"][:], [[-1, 128]],
                                             ALU.is_ge, -30000.0, channel_multiplier=1),
         wait=[("gv", zdone)], inc=["gg"])
    p.barrier()


def build_globals(nc, ctx):
    g = {}
    g["iotaf"] = ctx.enter_context(nc.sbuf_tensor("g_iotaf", [128, 128], F32))
    g["identb"] = ctx.enter_context(nc.sbuf_tensor("g_identb", [128, 128], BF16))
    g["ones128"] = ctx.enter_context(nc.sbuf_tensor("g_ones128", [128, 128], BF16))
    g["ones1"] = ctx.enter_context(nc.sbuf_tensor("g_ones1", [1, 512], BF16))
    g["onesf1"] = ctx.enter_context(nc.sbuf_tensor("g_onesf1", [1, 128], F32))
    g["maskf"] = ctx.enter_context(nc.sbuf_tensor("g_maskf", [128, 128], F32))
    g["identf32"] = ctx.enter_context(nc.sbuf_tensor("g_identf32", [128, 128], F32))
    return g


def build_program(phases=("p2",), p2_tb=512, p2_nt=None):
    nc = bass.Bass("TRN2", target_bir_lowering=False)
    ctx = contextlib.ExitStack()
    io = {}

    def dram(name, shape, kind):
        io[name] = nc.dram_tensor(name, shape, F32, kind=kind).ap()

    first = phases[0]
    stream_kind = {}
    order = ["x0", "x1", "x2", "x3"]
    pin = {"p1": "x0", "p2": "x1", "p3": "x2", "p4": "x3"}
    pout = {"p1": "x1", "p2": "x2", "p3": "x3", "p4": None}
    needed = set()
    for ph in phases:
        needed.add(pin[ph])
        if pout[ph]:
            needed.add(pout[ph])
    for sname in order:
        if sname not in needed:
            continue
        if sname == pin[first]:
            dram(sname, [D, T], "ExternalInput")
        elif any(pout[ph] == sname for ph in phases) and not any(pin[ph] == sname for ph in phases):
            dram(sname, [D, T], "ExternalOutput")
        else:
            dram(sname, [D, T], "Internal")
    if "p3" in phases:
        dram("x3t", [T, D], "Internal" if "p4" in phases else "ExternalOutput")
        dram("p3attn", [T, 16], "Internal" if not P4_DEBUG[0] else "ExternalOutput")
        if P4_DEBUG[0]:
            dram("p3dbg", [128, 112], "ExternalOutput")
    if "p4" in phases:
        if "p3" not in phases:
            dram("x3t", [T, D], "ExternalInput")
        dram("out", [T, D], "ExternalOutput")
        if P4_DEBUG[0]:
            dram("coefdbg", [128, (T // 128) * 8], "ExternalOutput")
            dram("gdbgo", [128, 32], "ExternalOutput")

    if "p1" in phases:
        for n, sh in [("l0_wq", [D, D]), ("l0_wk", [D, D]), ("l0_wv", [D, D]),
                      ("l0_wo", [D, D]), ("l0_bq", [128, 8]), ("l0_bk", [128, 8]),
                      ("l0_bv", [1, D]), ("l0_bo", [128, 8])]:
            dram(n, sh, "ExternalInput")
    if "p2" in phases:
        for n, sh in [("w1T", [D, DH]), ("w2T", [D, DH]), ("h1T", [DH, DH]),
                      ("h2T", [DH, DH]), ("w3T", [DH, D]),
                      ("w1_s", [128, 32]), ("w2_s", [128, 32]), ("h1_s", [128, 32]),
                      ("h2_s", [128, 32]), ("w3_s", [128, 8])]:
            dram(n, sh, "ExternalInput")
    if "p3" in phases:
        for n, sh in [("l1_wq", [D, 4 * D]), ("l1_wk", [D, 4 * D]), ("l1_wv", [D, 4 * D]),
                      ("l1_wo", [4 * D, D]), ("l1_ph", [1, 4 * D]), ("l1_bq", [128, 32]),
                      ("l1_bk", [128, 32]), ("l1_bv", [128, 32]), ("l1_bo", [128, 8])]:
            dram(n, sh, "ExternalInput")
    if "p4" in phases:
        for n, sh in [("wg", [D, NEXP]), ("bg", [1, NEXP]), ("m_w1", [NEXP, D, DH]),
                      ("m_b1", [NEXP, 128, 32]), ("m_w2", [NEXP, DH, D]),
                      ("m_b2", [NEXP, 1, D])]:
            dram(n, sh, "ExternalInput")

    p = P(nc, ctx)
    g = build_globals(nc, ctx)
    with nc.Block() as block:
        emit_globals(p, g)
        for ph in phases:
            if ph == "p2":
                emit_p2(p, io, g, TB=p2_tb, NT=p2_nt)
            elif ph == "p1":
                emit_p1(p, io, g, NSQ=(None if p2_nt is None else max(1, p2_nt // 4)))
            elif ph == "p3":
                emit_p3(p, io, g, NT=p2_nt)
            elif ph == "p4":
                emit_p4(p, io, g, NT=p2_nt, skip_ln=P4_DEBUG[0],
                        coef_out="coefdbg" if P4_DEBUG[0] else None)

        @block.tensor
        def _(e):
            for f in p.streams["tensor"]:
                f(e)

        @block.vector
        def _(e):
            for f in p.streams["vector"]:
                f(e)

        @block.scalar
        def _(e):
            for f in p.streams["scalar"]:
                f(e)

        @block.sync
        def _(e):
            for f in p.streams["sync"]:
                f(e)

        @block.gpsimd
        def _(e):
            for f in p.streams["gpsimd"]:
                f(e)
    ctx.close()
    return nc


def stage_shared(inputs):
    a = {}
    c = np.ascontiguousarray
    if "L0_attn_Wq" in inputs:
        a["l0_wq"] = c(inputs["L0_attn_Wq"].transpose(1, 0, 2).reshape(D, D))
        a["l0_wk"] = c(inputs["L0_attn_Wk"].transpose(1, 0, 2).reshape(D, D))
        a["l0_wv"] = c(inputs["L0_attn_Wv"].transpose(1, 0, 2).reshape(D, D))
        a["l0_wo"] = c(inputs["L0_attn_Wo"])
        a["l0_bq"] = c(inputs["L0_attn_bq"].reshape(-1).reshape(8, 128).T)
        a["l0_bk"] = c(inputs["L0_attn_bk"].reshape(-1).reshape(8, 128).T)
        a["l0_bv"] = c(inputs["L0_attn_bv"].reshape(1, D))
        a["l0_bo"] = c(inputs["L0_attn_bo"].reshape(8, 128).T)
    if "L0_ffn_w1_w" in inputs:
        a["w1T"] = c(inputs["L0_ffn_w1_w"].T)
        a["w2T"] = c(inputs["L0_ffn_w2_w"].T)
        a["h1T"] = c(inputs["L0_ffn_h1_w"].T)
        a["h2T"] = c(inputs["L0_ffn_h2_w"].T)
        a["w3T"] = c(inputs["L0_ffn_w3_w"].T)
        a["w1_s"] = c(inputs["L0_ffn_w1_s"].reshape(32, 128).T)
        a["w2_s"] = c(inputs["L0_ffn_w2_s"].reshape(32, 128).T)
        a["h1_s"] = c(inputs["L0_ffn_h1_s"].reshape(32, 128).T)
        a["h2_s"] = c(inputs["L0_ffn_h2_s"].reshape(32, 128).T)
        a["w3_s"] = c(inputs["L0_ffn_w3_s"].reshape(8, 128).T)
    if "L1_attn_Wq" in inputs:
        a["l1_wq"] = c(inputs["L1_attn_Wq"])
        a["l1_wk"] = c(inputs["L1_attn_Wk"])
        a["l1_wv"] = c(inputs["L1_attn_Wv"])
        a["l1_wo"] = c(inputs["L1_attn_Wo"])
        a["l1_ph"] = c(inputs["L1_attn_phase"].reshape(1, 4 * D))
        a["l1_bq"] = c(inputs["L1_attn_bq"].reshape(32, 128).T)
        a["l1_bk"] = c(inputs["L1_attn_bk"].reshape(32, 128).T)
        a["l1_bv"] = c(inputs["L1_attn_bv"].reshape(32, 128).T)
        a["l1_bo"] = c(inputs["L1_attn_bo"].reshape(8, 128).T)
    if "L1_ffn_Wg" in inputs:
        a["wg"] = c(inputs["L1_ffn_Wg"])
        a["bg"] = c(inputs["L1_ffn_bg"].reshape(1, NEXP))
        a["m_w1"] = c(inputs["L1_ffn_W1"])
        a["m_b1"] = c(inputs["L1_ffn_b1"].reshape(NEXP, 32, 128).transpose(0, 2, 1))
        a["m_w2"] = c(inputs["L1_ffn_W2"])
        a["m_b2"] = c(inputs["L1_ffn_b2"].reshape(NEXP, 1, D))
    return a


def stage_x_fm(x):
    return np.ascontiguousarray(x.transpose(2, 0, 1).reshape(D, -1))


FUSED = True

_P1KEYS = ["l0_wq", "l0_wk", "l0_wv", "l0_wo", "l0_bq", "l0_bk", "l0_bv", "l0_bo"]
_P2KEYS = ["w1T", "w2T", "h1T", "h2T", "w3T", "w1_s", "w2_s", "h1_s", "h2_s", "w3_s"]
_P3KEYS = ["l1_wq", "l1_wk", "l1_wv", "l1_wo", "l1_ph", "l1_bq", "l1_bk", "l1_bv", "l1_bo"]
_P4KEYS = ["wg", "bg", "m_w1", "m_b1", "m_w2", "m_b2"]


def kernel(**inputs):
    shared = stage_shared(inputs)
    x = inputs["L0_attn_x"]
    x0s = [stage_x_fm(x[c * NSEQ:(c + 1) * NSEQ]) for c in range(NCORES)]
    cores = list(range(NCORES))
    if FUSED:
        nc = build_program(phases=("p1", "p2", "p3", "p4"))
        in_maps = [dict(shared, x0=x0s[c]) for c in cores]
        res = run_bass_kernel_spmd(nc, in_maps, core_ids=cores)
        return np.concatenate([r["out"].reshape(NSEQ, S, D) for r in res.results], axis=0)
    nc1 = build_program(phases=("p1",))
    r1 = run_bass_kernel_spmd(
        nc1, [{k: shared[k] for k in _P1KEYS} | {"x0": x0s[c]} for c in cores], core_ids=cores)
    x1s = [r["x1"] for r in r1.results]
    nc2 = build_program(phases=("p2",))
    r2 = run_bass_kernel_spmd(
        nc2, [{k: shared[k] for k in _P2KEYS} | {"x1": x1s[c]} for c in cores], core_ids=cores)
    x2s = [r["x2"] for r in r2.results]
    nc3 = build_program(phases=("p3",))
    r3 = run_bass_kernel_spmd(
        nc3, [{k: shared[k] for k in _P3KEYS} | {"x2": x2s[c]} for c in cores], core_ids=cores)
    nc4 = build_program(phases=("p4",))
    r4 = run_bass_kernel_spmd(
        nc4, [{k: shared[k] for k in _P4KEYS} |
              {"x3": r3.results[c]["x3"], "x3t": r3.results[c]["x3t"]} for c in cores],
        core_ids=cores)
    return np.concatenate([r["out"].reshape(NSEQ, S, D) for r in r4.results], axis=0)


# version 04 (final)
# speedup vs baseline: 1.3473x; 1.3473x over previous
import contextlib
import numpy as np
import concourse.bass as bass
import concourse.mybir as mybir
from concourse.bass_utils import run_bass_kernel_spmd

F32 = mybir.dt.float32
BF16 = mybir.dt.bfloat16
AF = mybir.ActivationFunctionType
ALU = mybir.AluOpType
AX = mybir.AxisListType

NCORES = 8
NSEQ = 4
S = 2048
D = 1024
T = NSEQ * S
DH = 4096
NEXP = 8
EPS = 1e-5
ENGS = ("tensor", "vector", "scalar", "sync", "gpsimd")
P4_DEBUG = [False]


class P:

    def __init__(self, nc, ctx):
        self.nc = nc
        self.ctx = ctx
        self.streams = {e: [] for e in ENGS}
        self.sems = {}
        self.cnt = {}
        self.nbar = 0

    def sem(self, name):
        if name not in self.sems:
            self.sems[name] = self.ctx.enter_context(self.nc.semaphore(name))
            self.cnt[name] = 0
        return self.sems[name]

    def op(self, eng, fn, wait=(), inc=(), dma=False, ss=None):
        wait = list(wait)
        if ss is not None:
            wait.append((ss, self.c(ss)))
        for s, _ in wait:
            self.sem(s)
        incs = []
        for s in inc:
            self.sem(s)
            self.cnt[s] += 16 if dma else 1
            incs.append((self.sems[s], 16 if dma else 1))
        waits = [(self.sems[s], v) for (s, v) in wait]
        def run(e, waits=waits, fn=fn, incs=incs):
            for h, v in waits:
                e.wait_ge(h, v)
            inst = fn(e)
            for h, d in incs:
                inst = inst.then_inc(h, d)
        self.streams[eng].append(run)
        return self.cnt[inc[0]] if inc else None

    def c(self, s):
        self.sem(s)
        return self.cnt[s]

    def barrier(self):
        self.nbar += 1
        b = self.sem("bar")
        tgt = 5 * self.nbar
        for e in ENGS:
            def run(eng, b=b, tgt=tgt):
                eng.sem_inc(b, 1)
                eng.wait_ge(b, tgt)
            self.streams[e].append(run)
        self.cnt["bar"] = tgt


def _chunked(ap):
    return ap.rearrange("(c p) t -> p c t", p=128)


def emit_p2(p, io, g, TB=512, NT=None):
    nc = p.nc
    if NT is None:
        NT = T // TB
    NH = TB // 512
    pre = "p2"
    with contextlib.ExitStack() as ctx:
        sb = lambda n, sh, dt: ctx.enter_context(nc.sbuf_tensor(pre + n, sh, dt))
        psD = ctx.enter_context(nc.psum_tensor(pre + "psD", [128, 2048], F32))
        psA = ctx.enter_context(nc.psum_tensor(pre + "psA", [128, 512], F32))
        psB = ctx.enter_context(nc.psum_tensor(pre + "psB", [128, 512], F32))
        psC = ctx.enter_context(nc.psum_tensor(pre + "psC", [128, 512], F32))

        x1b = sb("x1b", [128, 8, TB], BF16)
        t1 = sb("t1", [128, 32, TB], BF16)
        t2 = sb("t2", [128, 32, TB], BF16)
        wstrip = [sb(f"ws{i}", [128, 8192], BF16) for i in range(2)]
        wv16 = [w.rearrange("p (a b) -> p a b", a=16) for w in wstrip]
        wv32 = [w.rearrange("p (a b) -> p a b", a=32) for w in wstrip]
        wv32c = [w.rearrange("p (a b) -> p a b", a=64) for w in wstrip]
        s1s = sb("s1s", [128, 32], F32)
        s2s = sb("s2s", [128, 32], F32)
        hs1s = sb("hs1s", [128, 32], F32)
        hs2s = sb("hs2s", [128, 32], F32)
        ws3s = sb("ws3s", [128, 8], F32)
        silu = [sb(f"sil{i}", [128, 512], F32) for i in range(2)]
        sbf = sb("sbf", [128, 8, TB], BF16)
        sqbf = sb("sqbf", [128, TB], BF16)
        Asc = sb("Asc", [128, TB], F32)
        Bsc = sb("Bsc", [128, TB], F32)
        tmp = sb("tmp", [128, TB], F32)
        x1f = [sb(f"x1f{i}", [128, TB], F32) for i in range(2)]
        x2f = [sb(f"x2f{i}", [128, TB], F32) for i in range(2)]

        for name, tile in [("w1_s", s1s), ("w2_s", s2s), ("h1_s", hs1s),
                           ("h2_s", hs2s), ("w3_s", ws3s)]:
            ap = io[name]
            p.op("sync", lambda e, t=tile, a=ap: e.dma_start(out=t[:], in_=a[:]),
                 inc=["p2s"], dma=True)
        scales_ld = p.c("p2s")

        strip_free = [("p2t", 0), ("p2t", 0)]
        silu_cons = [0, 0]
        sq_prev = [0]

        def strips_load(wname, view, par, nk, width, col0, row0=0):
            ap = _chunked(io[wname])
            ws, wv = strip_free[par]
            p.op("gpsimd",
                 lambda e, view=view, ap=ap, nk=nk, width=width, col0=col0, row0=row0:
                     e.dma_start(out=view[:, row0:row0 + nk, 0:width],
                                 in_=ap[:, 0:nk, col0:col0 + width]),
                 wait=[(ws, wv)] if wv > 0 else [], inc=["p2g"], dma=True)
            return p.c("p2g")

        for j in range(NT):
            jc = slice(j * TB, (j + 1) * TB)
            x1b_guard = [("p2t", p.c("p2t"))] if j > 0 else []
            p.op("gpsimd",
                 lambda e, jc=jc: e.dma_start(out=x1b[:], in_=_chunked(io["x1"])[:, :, jc]),
                 wait=x1b_guard, inc=["p2g"], dma=True)
            x1b_ld = p.c("p2g")

            for mg in range(8):
                par = mg % 2
                l1 = strips_load("w1T", wv16[par], par, 8, 512, mg * 512)
                strip_free[par] = (strip_free[par][0], 0)
                l2 = strips_load("w2T", wv16[par], par, 8, 512, mg * 512, row0=8)
                for m in range(4):
                    mi = mg * 4 + m
                    for nh in range(NH):
                        nhs = slice(nh * 512, (nh + 1) * 512)
                        for k in range(8):
                            p.op("tensor",
                                 lambda e, k=k, par=par, m=m, nh=nh, jc=jc, nhs=nhs:
                                     e.matmul(psA[:], wv16[par][:, k, m * 128:(m + 1) * 128],
                                              x1b[:, k, nhs],
                                              start=(k == 0), stop=(k == 7),
                                              skip_group_check=True),
                                 wait=([(("p2g"), l2), ("p2g", x1b_ld),
                                        ("p2a", p.c("p2a"))] if k == 0 else []),
                                 inc=(["p2t"] if k == 7 else []))
                        for k in range(8):
                            p.op("tensor",
                                 lambda e, k=k, par=par, m=m, nh=nh, nhs=nhs:
                                     e.matmul(psB[:], wv16[par][:, 8 + k, m * 128:(m + 1) * 128],
                                              x1b[:, k, nhs],
                                              start=(k == 0), stop=(k == 7),
                                              skip_group_check=True),
                                 wait=([("p2v", p.c("p2v"))] if k == 0 else []),
                                 inc=(["p2t"] if k == 7 else []))
                        tmm = p.c("p2t")
                        sbuf_i = (mi * NH + nh) % 2
                        p.op("scalar",
                             lambda e, sbuf_i=sbuf_i, mi=mi:
                                 e.activation(silu[sbuf_i][:], psA[:], AF.Silu,
                                              scale=s1s[:, mi:mi + 1]),
                             wait=[("p2t", tmm), ("p2s", scales_ld),
                                   ("p2v", silu_cons[sbuf_i])], inc=["p2a"])
                        amm = p.c("p2a")
                        p.op("vector",
                             lambda e, sbuf_i=sbuf_i, mi=mi, nhs=nhs:
                                 e.scalar_tensor_tensor(t1[:, mi, nhs], psB[:],
                                                        s2s[:, mi:mi + 1], silu[sbuf_i][:],
                                                        ALU.mult, ALU.mult),
                             wait=[("p2a", amm)], inc=["p2v"])
                        silu_cons[sbuf_i] = p.c("p2v")
                strip_free[par] = ("p2v", p.c("p2v"))

            for (wname, scl, src, dst) in [("h1T", hs1s, t1, t2), ("h2T", hs2s, t2, t1)]:
                for mg in range(16):
                    par = mg % 2
                    lh = strips_load(wname, wv32[par], par, 32, 256, mg * 256)
                    for m in range(2):
                        mi = mg * 2 + m
                        for nh in range(NH):
                            nhs = slice(nh * 512, (nh + 1) * 512)
                            ps = psA if (mi + nh) % 2 == 0 else psB
                            for k in range(32):
                                p.op("tensor",
                                     lambda e, k=k, par=par, m=m, nhs=nhs, ps=ps, src=src:
                                         e.matmul(ps[:], wv32[par][:, k, m * 128:(m + 1) * 128],
                                                  src[:, k, nhs],
                                                  start=(k == 0), stop=(k == 31),
                                                  skip_group_check=True),
                                     wait=([("p2g", lh), ("p2v", p.c("p2v")),
                                            ("p2a", p.c("p2a"))] if k == 0 else []),
                                     inc=(["p2t"] if k == 31 else []))
                            tmm = p.c("p2t")
                            p.op("scalar",
                                 lambda e, mi=mi, nhs=nhs, ps=ps, dst=dst, scl=scl:
                                     e.activation(dst[:, mi, nhs], ps[:], AF.Gelu,
                                                  scale=scl[:, mi:mi + 1]),
                                 wait=[("p2t", tmm)], inc=["p2a"])
                    strip_free[par] = ("p2a", p.c("p2a"))

            for m in range(8):
                par = m % 2
                lw = strips_load("w3T", wv32c[par], par, 32, 128, m * 128)
                p.op("sync",
                     lambda e, m=m, jc=jc, par=par:
                         e.dma_start(out=x1f[par][:], in_=io["x1"][m * 128:(m + 1) * 128, jc]),
                     wait=[("p2v", max(0, p.c("p2v")))] if (j > 0 or m >= 2) else [],
                     inc=["p2s"], dma=True)
                lx = p.c("p2s")
                for nh in range(NH):
                    nhs = slice(nh * 512, (nh + 1) * 512)
                    for k in range(32):
                        p.op("tensor",
                             lambda e, k=k, par=par, nhs=nhs:
                                 e.matmul(psC[:], wv32c[par][:, k, 0:128], t1[:, k, nhs],
                                          start=(k == 0), stop=(k == 31),
                                          skip_group_check=True),
                             wait=([("p2g", lw), ("p2v", p.c("p2v"))] if k == 0 else []),
                             inc=(["p2t"] if k == 31 else []))
                    tmm = p.c("p2t")
                    p.op("vector",
                         lambda e, m=m, nhs=nhs, par=par:
                             e.scalar_tensor_tensor(sbf[:, m, nhs], psC[:],
                                                    ws3s[:, m:m + 1], x1f[par][:, nhs],
                                                    ALU.mult, ALU.add),
                         wait=[("p2s", lx), ("p2t", tmm)], inc=["p2v"])
                strip_free[par] = ("p2v", p.c("p2v"))
                svm = p.c("p2v")
                p.op("scalar",
                     lambda e, m=m: e.activation(sqbf[:], sbf[:, m, :], AF.Square),
                     wait=[("p2v", svm), ("p2t", sq_prev[0])], inc=["p2a"])
                sqm = p.c("p2a")
                for nh in range(NH):
                    nhs = slice(nh * 512, (nh + 1) * 512)
                    p.op("tensor",
                         lambda e, m=m, nh=nh, nhs=nhs:
                             e.matmul(psD[:, nh * 512:nh * 512 + 512], g["ones128"][:],
                                      sbf[:, m, nhs], start=(m == 0), stop=(m == 7),
                                      skip_group_check=True),
                         wait=[("p2v", svm)] + ([("p2v", p.c("p2v"))] if m == 0 and nh == 0 else []),
                         inc=(["p2t"] if m == 7 and nh == NH - 1 else []))
                    p.op("tensor",
                         lambda e, m=m, nh=nh, nhs=nhs:
                             e.matmul(psD[:, 1024 + nh * 512:1024 + nh * 512 + 512],
                                      g["ones128"][:], sqbf[:, nhs], start=(m == 0),
                                      stop=(m == 7), skip_group_check=True),
                         wait=[("p2a", sqm)],
                         inc=["p2t"])
                    sq_prev[0] = p.c("p2t")
            tstat = p.c("p2t")
            p.op("vector", lambda e: e.tensor_scalar(Asc[:], psD[:, 0:TB], 1.0 / D, None, ALU.mult),
                 wait=[("p2t", tstat)], inc=["p2v"])
            p.op("vector", lambda e: e.tensor_scalar(tmp[:], psD[:, 1024:1024 + TB], 1.0 / D, None, ALU.mult),
                 inc=["p2v"])
            p.op("vector", lambda e: e.tensor_tensor(Bsc[:], Asc[:], Asc[:], ALU.mult), inc=["p2v"], ss="p2v")
            p.op("vector", lambda e: e.tensor_sub(tmp[:], tmp[:], Bsc[:]), inc=["p2v"], ss="p2v")
            p.op("vector", lambda e: e.tensor_scalar(tmp[:], tmp[:], EPS, None, ALU.add), inc=["p2v"], ss="p2v")
            p.op("vector", lambda e: e.reciprocal(tmp[:], tmp[:]), inc=["p2v"], ss="p2v")
            mean_done = p.c("p2v")
            p.op("scalar", lambda e: e.activation(tmp[:], tmp[:], AF.Sqrt),
                 wait=[("p2v", mean_done)], inc=["p2a"])
            rstd_done = p.c("p2a")
            p.op("vector", lambda e: e.scalar_tensor_tensor(Bsc[:], Asc[:], -1.0, tmp[:], ALU.mult, ALU.mult),
                 wait=[("p2a", rstd_done)], inc=["p2v"], ss="p2v")
            for m in range(8):
                buf = x2f[m % 2]
                st_guard = [("p2s", p.c("p2s"))] if m >= 2 else []
                p.op("vector",
                     lambda e, m=m, buf=buf: e.tensor_tensor(buf[:], sbf[:, m, :], tmp[:], ALU.mult),
                     wait=st_guard, inc=["p2v"], ss="p2v")
                p.op("vector",
                     lambda e, m=m, buf=buf: e.tensor_add(buf[:], buf[:], Bsc[:]),
                     inc=["p2v"], ss="p2v")
                vd = p.c("p2v")
                p.op("sync",
                     lambda e, m=m, jc=jc, buf=buf:
                         e.dma_start(out=io["x2"][m * 128:(m + 1) * 128, jc], in_=buf[:]),
                     wait=[("p2v", vd)], inc=["p2s"], dma=True)
        p.op("sync", lambda e: None, wait=[("p2s", p.c("p2s"))])
    p.barrier()


def emit_p1(p, io, g, NSQ=None):
    nc = p.nc
    nseq = NSEQ if NSQ is None else NSQ
    pre = "p1"
    with contextlib.ExitStack() as ctx:
        sb = lambda n, sh, dt: ctx.enter_context(nc.sbuf_tensor(pre + n, sh, dt))
        psA = ctx.enter_context(nc.psum_tensor(pre + "psA", [128, 512], F32))
        psC = ctx.enter_context(nc.psum_tensor(pre + "psC", [128, 512], F32))
        psD = ctx.enter_context(nc.psum_tensor(pre + "psD", [128, 2048], F32))
        psO = ctx.enter_context(nc.psum_tensor(pre + "psO", [128, 512], F32))
        psR = ctx.enter_context(nc.psum_tensor(pre + "psR", [128, 512], F32))

        warena = sb("warena", [128, 8, 3072], BF16)
        wq = warena[:, :, 0:1024]
        wk = warena[:, :, 1024:2048]
        wv = warena[:, :, 2048:3072]
        wo = warena[:, :, 0:1024]
        bqs = sb("bqs", [128, 8], F32)
        bks = sb("bks", [128, 8], F32)
        bvr = sb("bvr", [1, 1024], BF16)
        bos = sb("bos", [128, 8], F32)
        xsb_u = sb("xsb", [128, 8, 512], BF16)
        x0b = xsb_u
        Q = sb("Q", [128, 8, 2048], BF16)
        K = sb("K", [128, 8, 2048], BF16)
        Vx = sb("Vx", [128, 16, 1040], BF16)
        AO = sb("AO", [128, 8, 2048], BF16)
        PTbig = sb("PTbig", [128, 2048], BF16)
        cmaskM = sb("cmM", [128, 896], BF16)
        cmask = [cmaskM[:, (3 - r) * 128:(3 - r) * 128 + 512] for r in range(4)]
        onesr = sb("onesr", [128, 128], F32)
        onescol = sb("onescol", [128, 1], BF16)
        shiftb = sb("shiftb", [128, 1], F32)
        sbf = xsb_u
        sqbf = sb("sqbf", [128, 512], BF16)
        Asc = sb("Asc", [128, 512], F32)
        Bsc = sb("Bsc", [128, 512], F32)
        tmp = sb("tmp", [128, 512], F32)
        x0f = [sb(f"x0f{i}", [128, 512], F32) for i in range(2)]
        x1f = x0f
        rrow = Asc
        rrep = Bsc

        vxall = Vx[:].rearrange("p k (a2 par c) -> p k a2 par c", par=2, c=65)
        p.op("vector", lambda e: e.memset(vxall[:, :, :, 0, 64:65], 1.0), inc=["p1v"])
        p.op("vector", lambda e: e.memset(vxall[:, :, :, 1, 0:1], 1.0), inc=["p1v"])
        p.op("vector", lambda e: e.memset(onesr[:], 1.0), inc=["p1v"])
        p.op("vector", lambda e: e.memset(onescol[:], 1.0), inc=["p1v"])
        p.op("vector", lambda e: e.memset(shiftb[:], -12.0), inc=["p1v"])
        p.op("vector", lambda e: e.memset(cmaskM[:], 0.0), inc=["p1v"])
        zm = p.c("p1v")
        p.op("gpsimd",
             lambda e: e.affine_select(cmaskM[:], cmaskM[:], [[1, 896]],
                                       ALU.is_ge, -30000.0, base=-384,
                                       channel_multiplier=-1),
             wait=[("p1v", zm)], inc=["p1g"], dma=False)
        mask_done = p.c("p1g")

        p.op("gpsimd", lambda e: e.dma_start(out=bvr[:], in_=io["l0_bv"][:]),
             inc=["p1g"], dma=True)
        for name, tile in [("l0_bq", bqs), ("l0_bk", bks), ("l0_bo", bos)]:
            p.op("sync", lambda e, t=tile, a=io[name]: e.dma_start(out=t[:], in_=a[:]),
                 inc=["p1s"], dma=True)
        b_ld = p.c("p1s")

        sq_prev = [0]
        for b in range(nseq):
            sb0 = b * S
            for name, tile in [("l0_wq", wq), ("l0_wk", wk), ("l0_wv", wv)]:
                p.op("gpsimd", lambda e, t=tile, a=io[name]:
                         e.dma_start(out=t[:], in_=_chunked(a)[:]),
                     wait=[("p1t", p.c("p1t"))], inc=["p1g"], dma=True)
            w_ld = p.c("p1g")
            for tb in range(4):
                tc = slice(sb0 + tb * 512, sb0 + (tb + 1) * 512)
                p.op("gpsimd",
                     lambda e, tc=tc: e.dma_start(out=x0b[:], in_=_chunked(io["x0"])[:, :, tc]),
                     wait=([("p1t", p.c("p1t")), ("p1v", p.c("p1v"))] if (b or tb) else []),
                     inc=["p1g"], dma=True)
                xld = p.c("p1g")
                for (W, dst, bias) in [(wq, Q, bqs), (wk, K, bks)]:
                    for g8 in range(8):
                        ps = psA if g8 % 2 == 0 else psC
                        for k in range(8):
                            p.op("tensor",
                                 lambda e, W=W, g8=g8, k=k, ps=ps:
                                     e.matmul(ps[:], W[:, k, g8 * 128:(g8 + 1) * 128],
                                              x0b[:, k, :], start=(k == 0), stop=(k == 7),
                                              skip_group_check=True),
                                 wait=([("p1g", max(xld, w_ld)), ("p1a", p.c("p1a"))]
                                       if k == 0 else []),
                                 inc=(["p1t"] if k == 7 else []))
                        tmm = p.c("p1t")
                        p.op("scalar",
                             lambda e, dst=dst, g8=g8, tb=tb, ps=ps, bias=bias:
                                 e.activation(dst[:, g8, tb * 512:(tb + 1) * 512], ps[:],
                                              AF.Identity, bias=bias[:, g8:g8 + 1]),
                             wait=[("p1t", tmm), ("p1s", b_ld)], inc=["p1a"])
                for ts in range(4):
                    for nh in range(2):
                        pd = psD[:, (ts % 2) * 1024 + nh * 512:(ts % 2) * 1024 + (nh + 1) * 512]
                        for k in range(8):
                            p.op("tensor",
                                 lambda e, ts=ts, nh=nh, k=k, pd=pd:
                                     e.matmul(pd, x0b[:, k, ts * 128:(ts + 1) * 128],
                                              wv[:, k, nh * 512:(nh + 1) * 512],
                                              start=(k == 0), stop=False,
                                              skip_group_check=True),
                                 wait=([("p1v", p.c("p1v"))] if k == 0 else []),
                                 inc=[])
                        p.op("tensor",
                             lambda e, nh=nh, pd=pd:
                                 e.matmul(pd, g["ones1"][0:1, 0:128],
                                          bvr[:, nh * 512:(nh + 1) * 512],
                                          start=False, stop=True, skip_group_check=True),
                             inc=["p1t"])
                        vmm = p.c("p1t")
                        kt = tb * 4 + ts
                        vxv = Vx[:, kt, :].rearrange("p (a2 par c) -> p a2 par c",
                                                     par=2, c=65)
                        pdv = pd.rearrange("p (a par d) -> p a par d", par=2, d=64)
                        p.op("vector",
                             lambda e, vxv=vxv, pdv=pdv, nh=nh:
                                 e.tensor_copy(vxv[:, nh * 4:(nh + 1) * 4, 0, 0:64],
                                               pdv[:, :, 0, :]),
                             wait=[("p1t", vmm)], inc=["p1v"])
                        p.op("vector",
                             lambda e, vxv=vxv, pdv=pdv, nh=nh:
                                 e.tensor_copy(vxv[:, nh * 4:(nh + 1) * 4, 1, 1:65],
                                               pdv[:, :, 1, :]),
                             inc=["p1v"])
            p.op("gpsimd", lambda e: e.dma_start(out=wo[:], in_=_chunked(io["l0_wo"])[:]),
                 wait=[("p1t", p.c("p1t"))], inc=["p1g"], dma=True)
            wo_ld = p.c("p1g")
            qkv_a = p.c("p1a")
            qkv_v = p.c("p1v")

            SHIFT = 12.0
            sbank = [psD[:, i * 512:(i + 1) * 512] for i in range(4)]
            grp_free = p.c("p1a")
            pt_free = p.c("p1t")
            psO_free = p.c("p1v")
            psR_free = p.c("p1v")
            rrep_free = p.c("p1v")
            for qt in range(4):
                nkb = 4 * qt + 4
                for hd in range(16):
                    g8, hh = hd // 2, hd % 2
                    hp = slice(64 * hh, 64 * hh + 64)
                    avr = slice(0, 64) if hh == 0 else slice(64, 128)
                    for gi in range(qt + 1):
                        for j4 in range(4):
                            kb = 4 * gi + j4
                            w0 = []
                            if j4 == 0:
                                w0 = [("p1a", max(qkv_a, grp_free))]
                                if qt == 0 and hd == 0:
                                    w0 += [("p1g", mask_done), ("p1v", qkv_v)]
                            p.op("tensor",
                                 lambda e, g8=g8, hp=hp, kb=kb, qt=qt, j4=j4:
                                     e.matmul(sbank[j4], K[hp, g8, kb * 128:(kb + 1) * 128],
                                              Q[hp, g8, qt * 512:(qt + 1) * 512],
                                              start=True, stop=True, skip_group_check=True),
                                 wait=w0, inc=(["p1t"] if j4 == 3 else []))
                        smm = p.c("p1t")
                        ew = [("p1t", smm)]
                        if gi == qt:
                            for r in range(4):
                                p.op("vector",
                                     lambda e, r=r: e.tensor_add(sbank[r], sbank[r],
                                                                 cmask[r]),
                                     wait=([("p1t", smm)] if r == 0 else []),
                                     inc=(["p1v"] if r == 3 else []))
                            ew = [("p1v", p.c("p1v"))]
                        p.op("scalar",
                             lambda e: e.activation(PTbig[:], psD[:], AF.Exp,
                                                    bias=shiftb[:, 0:1], scale=0.125),
                             wait=ew + [("p1t", pt_free)], inc=["p1a"])
                        grp_free = p.c("p1a")
                        for j4 in range(4):
                            kb = 4 * gi + j4
                            w1 = [("p1a", grp_free)] if j4 == 0 else []
                            if kb == 0:
                                w1.append(("p1v", max(qkv_v, psO_free)))
                            pt = PTbig[:, j4 * 512:(j4 + 1) * 512]
                            if hh == 0:
                                p.op("tensor",
                                     lambda e, kb=kb, hd=hd, pt=pt, nkb=nkb:
                                         e.matmul(psO[0:65, :],
                                                  Vx[:, kb, hd * 65:hd * 65 + 65],
                                                  pt, start=(kb == 0), stop=(kb == nkb - 1),
                                                  skip_group_check=True),
                                     wait=w1, inc=(["p1t"] if j4 == 3 else []))
                            else:
                                p.op("tensor",
                                     lambda e, kb=kb, hd=hd, pt=pt, nkb=nkb:
                                         e.matmul(psO[64:128, :],
                                                  Vx[:, kb, hd * 65 + 1:hd * 65 + 65],
                                                  pt, start=(kb == 0), stop=(kb == nkb - 1),
                                                  skip_group_check=True),
                                     wait=w1, inc=[])
                                p.op("tensor",
                                     lambda e, kb=kb, pt=pt, nkb=nkb:
                                         e.matmul(psR[0:1, :], onescol[:, 0:1], pt,
                                                  start=(kb == 0), stop=(kb == nkb - 1),
                                                  skip_group_check=True),
                                     wait=([("p1v", psR_free)] if kb == 0 else []),
                                     inc=(["p1t"] if j4 == 3 else []))
                        pt_free = p.c("p1t")
                    avd = p.c("p1t")
                    sumr = 64 if hh == 0 else 0
                    p.op("vector",
                         lambda e, sumr=sumr, hh=hh:
                             e.tensor_copy(rrow[sumr:sumr + 1, :],
                                           (psO if hh == 0 else psR)[sumr:sumr + 1, :]),
                         wait=[("p1t", avd), ("p1v", rrep_free)], inc=["p1v"])
                    rd = p.c("p1v")
                    p.op("tensor",
                         lambda e, sumr=sumr: e.matmul(psR[:], onesr[sumr:sumr + 1, :],
                                                       rrow[sumr:sumr + 1, :],
                                                       start=True, stop=True,
                                                       skip_group_check=True),
                         wait=[("p1v", max(rd, psR_free))], inc=["p1t"])
                    repd = p.c("p1t")
                    p.op("vector",
                         lambda e, avr=avr: e.reciprocal(rrep[avr, :], psR[avr, :]),
                         wait=[("p1t", repd), ("p1v", rrep_free)], inc=["p1v"])
                    psR_free = p.c("p1v")
                    p.op("vector",
                         lambda e, avr=avr, g8=g8, qt=qt:
                             e.tensor_tensor(AO[avr, g8, qt * 512:(qt + 1) * 512],
                                             psO[avr, :], rrep[avr, :], ALU.mult),
                         inc=["p1v"], ss="p1v")
                    psO_free = rrep_free = p.c("p1v")
            attn_v = p.c("p1v")

            for nb in range(4):
                jc = slice(sb0 + nb * 512, sb0 + (nb + 1) * 512)
                for m in range(8):
                    ps = psA if m % 2 == 0 else psC
                    p.op("sync",
                         lambda e, m=m, jc=jc, par=m % 2:
                             e.dma_start(out=x0f[par][:], in_=io["x0"][m * 128:(m + 1) * 128, jc]),
                         wait=([("p1v", p.c("p1v")), ("p1s", p.c("p1s"))]
                               if (b or nb or m >= 2) else []),
                         inc=["p1s"], dma=True)
                    lx = p.c("p1s")
                    for k in range(8):
                        p.op("tensor",
                             lambda e, k=k, m=m, nb=nb, ps=ps:
                                 e.matmul(ps[:], wo[:, k, m * 128:(m + 1) * 128],
                                          AO[:, k, nb * 512:(nb + 1) * 512],
                                          start=(k == 0), stop=(k == 7), skip_group_check=True),
                             wait=([("p1v", max(attn_v, p.c("p1v"))), ("p1g", wo_ld)]
                                   if k == 0 else []),
                             inc=(["p1t"] if k == 7 else []))
                    tmm = p.c("p1t")
                    p.op("vector",
                         lambda e, m=m, par=m % 2, ps=ps:
                             e.scalar_tensor_tensor(sbf[:, m, :], ps[:], bos[:, m:m + 1],
                                                    x0f[par][:], ALU.add, ALU.add),
                         wait=[("p1s", lx), ("p1t", tmm)], inc=["p1v"])
                    svm = p.c("p1v")
                    p.op("scalar",
                         lambda e, m=m: e.activation(sqbf[:], sbf[:, m, :], AF.Square),
                         wait=[("p1v", svm), ("p1t", sq_prev[0])], inc=["p1a"])
                    sqm = p.c("p1a")
                    p.op("tensor",
                         lambda e, m=m: e.matmul(psD[:, 0:512], g["ones128"][:], sbf[:, m, :],
                                                 start=(m == 0), stop=(m == 7),
                                                 skip_group_check=True),
                         wait=[("p1v", svm)] + ([("p1v", p.c("p1v"))] if m == 0 else []),
                         inc=(["p1t"] if m == 7 else []))
                    p.op("tensor",
                         lambda e, m=m: e.matmul(psD[:, 512:1024], g["ones128"][:], sqbf[:],
                                                 start=(m == 0), stop=(m == 7),
                                                 skip_group_check=True),
                         wait=[("p1a", sqm)], inc=["p1t"])
                    sq_prev[0] = p.c("p1t")
                tstat = p.c("p1t")
                p.op("vector", lambda e: e.tensor_scalar(Asc[:], psD[:, 0:512], 1.0 / D, None,
                                                         ALU.mult),
                     wait=[("p1t", tstat)], inc=["p1v"])
                p.op("vector", lambda e: e.tensor_scalar(tmp[:], psD[:, 512:1024], 1.0 / D, None,
                                                         ALU.mult), inc=["p1v"])
                p.op("vector", lambda e: e.tensor_tensor(Bsc[:], Asc[:], Asc[:], ALU.mult),
                     inc=["p1v"], ss="p1v")
                p.op("vector", lambda e: e.tensor_sub(tmp[:], tmp[:], Bsc[:]), inc=["p1v"], ss="p1v")
                p.op("vector", lambda e: e.tensor_scalar(tmp[:], tmp[:], EPS, None, ALU.add),
                     inc=["p1v"], ss="p1v")
                p.op("vector", lambda e: e.reciprocal(tmp[:], tmp[:]), inc=["p1v"], ss="p1v")
                md = p.c("p1v")
                p.op("scalar", lambda e: e.activation(tmp[:], tmp[:], AF.Sqrt),
                     wait=[("p1v", md)], inc=["p1a"], ss="p1a")
                rsd = p.c("p1a")
                p.op("vector", lambda e: e.scalar_tensor_tensor(Bsc[:], Asc[:], -1.0, tmp[:],
                                                                ALU.mult, ALU.mult),
                     wait=[("p1a", rsd)], inc=["p1v"], ss="p1v")
                for m in range(8):
                    buf = x1f[m % 2]
                    stg = [("p1s", p.c("p1s"))] if (nb or m >= 2) else []
                    p.op("vector",
                         lambda e, m=m, buf=buf: e.tensor_tensor(buf[:], sbf[:, m, :], tmp[:],
                                                                 ALU.mult),
                         wait=stg, inc=["p1v"], ss="p1v")
                    p.op("vector", lambda e, buf=buf: e.tensor_add(buf[:], buf[:], Bsc[:]),
                         inc=["p1v"], ss="p1v")
                    vd = p.c("p1v")
                    p.op("sync",
                         lambda e, m=m, jc=jc, buf=buf:
                             e.dma_start(out=io["x1"][m * 128:(m + 1) * 128, jc], in_=buf[:]),
                         wait=[("p1v", vd)], inc=["p1s"], dma=True)
        p.op("sync", lambda e: None, wait=[("p1s", p.c("p1s"))])
    p.barrier()


def emit_p3(p, io, g, NT=None):
    emit_p3a(p, io, g, NT=NT)
    emit_p3b(p, io, g, NT=NT)


def emit_p3a(p, io, g, NT=None):
    nc = p.nc
    NB = (T // 512) if NT is None else NT
    pre = "p3"
    with contextlib.ExitStack() as ctx:
        sb = lambda n, sh, dt: ctx.enter_context(nc.sbuf_tensor(pre + n, sh, dt))
        psA = ctx.enter_context(nc.psum_tensor(pre + "psA", [128, 512], F32))
        psB = ctx.enter_context(nc.psum_tensor(pre + "psB", [128, 512], F32))
        psU = [ctx.enter_context(nc.psum_tensor(pre + f"psU{i}", [128, 1024], BF16))
               for i in range(2)]

        x2b = sb("x2b", [128, 8, 512], BF16)
        qkfm = sb("qkfm", [128, 512], BF16)
        qt = sb("qt", [128, 4, 4096], BF16)
        kt = sb("kt", [128, 4, 4096], BF16)
        qr = sb("qr", [128, 4096], BF16)
        qi = sb("qi", [128, 4096], BF16)
        kr = sb("kr", [128, 4096], BF16)
        ki = sb("ki", [128, 4096], BF16)
        crep = sb("crep", [128, 4096], BF16)
        srep = sb("srep", [128, 4096], BF16)
        zpair = [sb(f"zp{i}", [128, 1024], BF16) for i in range(2)]
        zipair = [sb(f"zi{i}", [128, 1024], BF16) for i in range(2)]
        sr = sb("sr", [128, 16], F32)
        smx = sb("smx", [128, 4], F32)
        ssum = sb("ssum", [128, 4], F32)
        attn = [sb(f"attn{i}", [128, 16], F32) for i in range(2)]
        p3dbg = sb("p3dbg", [128, 112], F32) if "p3dbg" in io else None
        wstrip = [sb(f"ws{i}", [128, 8, 512], BF16) for i in range(2)]
        bqs = sb("bqs", [128, 32], F32)
        bks = sb("bks", [128, 32], F32)
        phc = sb("phc", [1, 512], F32)
        trow = [sb(f"trow{i}", [1, 512], F32) for i in range(2)]
        csrow = sb("csrow", [1, 1024], BF16)

        for name, tile in [("l1_bq", bqs), ("l1_bk", bks)]:
            p.op("sync", lambda e, t=tile, a=io[name]: e.dma_start(out=t[:], in_=a[:]),
                 inc=["p3s"], dma=True)
        M_RND = 12582912.0
        for c8 in range(8):
            gd = [("p3v", p.c("p3v"))] if c8 > 0 else []
            p.op("sync", lambda e, c8=c8: e.dma_start(out=phc[:],
                                                      in_=io["l1_ph"][:, c8 * 512:(c8 + 1) * 512]),
                 wait=gd, inc=["p3s"], dma=True)
            pld = p.c("p3s")
            p.op("vector", lambda e: e.tensor_scalar(trow[0][:], phc[:], 1.0 / (2 * np.pi), M_RND,
                                                     ALU.mult, ALU.add),
                 wait=[("p3s", pld), ("p3a", p.c("p3a"))], inc=["p3v"], ss="p3v")
            p.op("vector", lambda e: e.tensor_scalar(trow[0][:], trow[0][:], M_RND, None,
                                                     ALU.subtract), inc=["p3v"], ss="p3v")
            p.op("vector", lambda e: e.scalar_tensor_tensor(trow[1][:], trow[0][:], -2 * np.pi,
                                                            phc[:], ALU.mult, ALU.add),
                 inc=["p3v"], ss="p3v")
            svd = p.c("p3v")
            p.op("scalar", lambda e: e.activation(csrow[:, 512:1024], trow[1][:], AF.Sin),
                 wait=[("p3v", svd)], inc=["p3a"], ss="p3a")
            p.op("vector", lambda e: e.tensor_scalar(trow[1][:], phc[:], -1.0 / (2 * np.pi), 0.25,
                                                     ALU.mult, ALU.add),
                 wait=[("p3a", p.c("p3a"))], inc=["p3v"], ss="p3v")
            p.op("vector", lambda e: e.tensor_scalar(trow[0][:], trow[1][:], M_RND, None,
                                                     ALU.add), inc=["p3v"], ss="p3v")
            p.op("vector", lambda e: e.tensor_scalar(trow[0][:], trow[0][:], M_RND, None,
                                                     ALU.subtract), inc=["p3v"], ss="p3v")
            p.op("vector", lambda e: e.tensor_sub(trow[1][:], trow[1][:], trow[0][:]),
                 inc=["p3v"], ss="p3v")
            p.op("vector", lambda e: e.tensor_scalar(trow[1][:], trow[1][:], 2 * np.pi, None,
                                                     ALU.mult), inc=["p3v"], ss="p3v")
            cvd = p.c("p3v")
            p.op("scalar", lambda e: e.activation(csrow[:, 0:512], trow[1][:], AF.Sin),
                 wait=[("p3v", cvd)], inc=["p3a"], ss="p3a")
            rd = p.c("p3a")
            for (off, rep) in [(0, crep), (512, srep)]:
                ps = psA if off == 0 else psB
                p.op("tensor",
                     lambda e, off=off, ps=ps:
                         e.matmul(ps[:], g["ones1"][0:1, 0:128], csrow[:, off:off + 512],
                                  start=True, stop=True, skip_group_check=True),
                     wait=[("p3a", rd), ("p3v", p.c("p3v"))], inc=["p3t"])
                tmm = p.c("p3t")
                p.op("vector",
                     lambda e, rep=rep, c8=c8, ps=ps:
                         e.tensor_copy(rep[:, c8 * 512:(c8 + 1) * 512], ps[:]),
                     wait=[("p3t", tmm)], inc=["p3v"])
        rep_done = p.c("p3v")

        strip_free = [0, 0]
        for j in range(NB):
            jc = slice(j * 512, (j + 1) * 512)
            xguard = [("p3t", p.c("p3t"))] if j > 0 else []
            p.op("gpsimd", lambda e, jc=jc: e.dma_start(out=x2b[:], in_=_chunked(io["x2"])[:, :, jc]),
                 wait=xguard, inc=["p3g"], dma=True)
            xld = p.c("p3g")
            for (wname, bias, dst) in [("l1_wq", bqs, qt), ("l1_wk", bks, kt)]:
                for mg in range(8):
                    par = mg % 2
                    p.op("gpsimd",
                         lambda e, wname=wname, mg=mg, par=par:
                             e.dma_start(out=wstrip[par][:],
                                         in_=_chunked(io[wname])[:, :, mg * 512:(mg + 1) * 512]),
                         wait=[("p3t", strip_free[par])], inc=["p3g"], dma=True)
                    wld = p.c("p3g")
                    for m in range(4):
                        mo = mg * 4 + m
                        ps = psA if mo % 2 == 0 else psB
                        for k in range(8):
                            p.op("tensor",
                                 lambda e, k=k, par=par, m=m, ps=ps:
                                     e.matmul(ps[:], wstrip[par][:, k, m * 128:(m + 1) * 128],
                                              x2b[:, k, :], start=(k == 0), stop=(k == 7),
                                              skip_group_check=True),
                                 wait=([("p3g", wld), ("p3g", xld), ("p3a", p.c("p3a"))]
                                       if k == 0 else []),
                                 inc=(["p3t"] if k == 7 else []))
                        p.op("scalar",
                             lambda e, mo=mo, ps=ps, bias=bias:
                                 e.activation(qkfm[:], ps[:], AF.Identity,
                                              bias=bias[:, mo:mo + 1]),
                             wait=[("p3t", p.c("p3t"))], inc=["p3a"])
                        fmd = p.c("p3a")
                        for ts in range(4):
                            pu = psU[ts % 2]
                            p.op("tensor",
                                 lambda e, ts=ts, pu=pu:
                                     e.transpose(pu[:, 0:128], qkfm[:, ts * 128:(ts + 1) * 128],
                                                 g["identb"][:]),
                                 wait=[("p3a", fmd), ("p3v", p.c("p3v"))], inc=["p3t"])
                            trd = p.c("p3t")
                            p.op("vector",
                                 lambda e, ts=ts, mo=mo, dst=dst, pu=pu:
                                     e.tensor_copy(dst[:, ts, mo * 128:(mo + 1) * 128],
                                                   pu[:, 0:128]),
                                 wait=[("p3t", trd)], inc=["p3v"])
                    strip_free[par] = p.c("p3t")
            qkv_done_v = p.c("p3v")

            for ts in range(4):
                t0 = j * 4 + ts
                for (dstt, srct, rep) in [(qr, qt, crep), (qi, qt, srep),
                                          (kr, kt, crep), (ki, kt, srep)]:
                    p.op("vector",
                         lambda e, dstt=dstt, srct=srct, rep=rep, ts=ts:
                             e.tensor_mul(dstt[:], srct[:, ts, :], rep[:]),
                         wait=([("p3v", rep_done)] if j == 0 and ts == 0 else []),
                         inc=["p3v"], ss=("p3v" if ts == 0 else None))
                mul_done = p.c("p3v")
                for g_ in range(4):
                    for h_ in range(4):
                        par2 = (g_ * 4 + h_) % 2
                        zz = zpair[par2]
                        zi = zipair[par2]
                        aguard = [("p3a", p.c("p3a"))] if (g_ * 4 + h_) >= 2 else []
                        p.op("vector",
                             lambda e, g_=g_, h_=h_, zz=zz:
                                 e.tensor_mul(zz[:], qr[:, h_ * 1024:(h_ + 1) * 1024],
                                              kr[:, g_ * 1024:(g_ + 1) * 1024]),
                             wait=aguard, inc=["p3v"],
                             ss=("p3v" if g_ == 0 and h_ == 0 else None))
                        p.op("vector",
                             lambda e, g_=g_, h_=h_, zi=zi:
                                 e.tensor_mul(zi[:], qi[:, h_ * 1024:(h_ + 1) * 1024],
                                              ki[:, g_ * 1024:(g_ + 1) * 1024]),
                             inc=["p3v"])
                        p.op("vector",
                             lambda e, zz=zz, zi=zi: e.tensor_add(zz[:], zz[:], zi[:]),
                             inc=["p3v"], ss="p3v")
                        zd = p.c("p3v")
                        p.op("scalar",
                             lambda e, g_=g_, h_=h_, zz=zz:
                                 e.activation(zz[:], zz[:], AF.Identity, scale=1.0 / 32,
                                              accum_out=sr[:, h_ * 4 + g_:h_ * 4 + g_ + 1]),
                             wait=[("p3v", zd)], inc=["p3a"],
                             ss=("p3a" if (g_ * 4 + h_) < 2 else None))
                sdots = p.c("p3a")
                if p3dbg is not None and j == 0 and ts == 0:
                    p.op("vector", lambda e: e.tensor_copy(p3dbg[:, 0:16], sr[:]),
                         wait=[("p3a", sdots)], inc=["p3v"])
                    p.op("vector", lambda e: e.tensor_copy(p3dbg[:, 16:32], qt[:, 0, 0:16]), inc=["p3v"])
                    p.op("vector", lambda e: e.tensor_copy(p3dbg[:, 32:48], kt[:, 0, 0:16]), inc=["p3v"])
                    p.op("vector", lambda e: e.tensor_copy(p3dbg[:, 48:64], crep[:, 0:16]), inc=["p3v"])
                    p.op("vector", lambda e: e.tensor_copy(p3dbg[:, 64:80], srep[:, 0:16]), inc=["p3v"])
                    p.op("vector", lambda e: e.tensor_copy(p3dbg[:, 80:96], qr[:, 0:16]), inc=["p3v"])
                    p.op("vector", lambda e: e.tensor_copy(p3dbg[:, 96:112], kr[:, 0:16]), inc=["p3v"])
                sr3 = sr[:].rearrange("p (h g2) -> p h g2", h=4)
                p.op("vector", lambda e, sr3=sr3: e.reduce_max(smx[:], sr3, axis=AX.X),
                     wait=[("p3a", sdots)], inc=["p3v"], ss="p3v")
                p.op("vector",
                     lambda e, sr3=sr3: e.tensor_tensor(
                         sr3, sr3, smx[:].unsqueeze(2).broadcast_to([128, 4, 4]), ALU.subtract),
                     inc=["p3v"], ss="p3v")
                evd = p.c("p3v")
                p.op("scalar", lambda e: e.activation(sr[:], sr[:], AF.Exp),
                     wait=[("p3v", evd)], inc=["p3a"], ss="p3a")
                exd = p.c("p3a")
                p.op("vector", lambda e, sr3=sr3: e.reduce_sum(ssum[:], sr3, axis=AX.X),
                     wait=[("p3a", exd)], inc=["p3v"], ss="p3v")
                p.op("vector", lambda e: e.reciprocal(ssum[:], ssum[:]), inc=["p3v"], ss="p3v")
                ab = attn[ts % 2]
                p.op("vector",
                     lambda e, sr3=sr3, ab=ab: e.tensor_tensor(
                         ab[:].rearrange("p (h g2) -> p h g2", h=4), sr3,
                         ssum[:].unsqueeze(2).broadcast_to([128, 4, 4]), ALU.mult),
                     wait=[("p3s", p.c("p3s"))], inc=["p3v"], ss="p3v")
                atd = p.c("p3v")
                p.op("sync",
                     lambda e, t0=t0, ab=ab:
                         e.dma_start(out=io["p3attn"][t0 * 128:(t0 + 1) * 128, :], in_=ab[:]),
                     wait=[("p3v", atd)], inc=["p3s"], dma=True)
        if p3dbg is not None:
            p.op("sync", lambda e: e.dma_start(out=io["p3dbg"][:], in_=p3dbg[:]),
                 wait=[("p3v", p.c("p3v"))], inc=["p3s"], dma=True)
        p.op("sync", lambda e: None, wait=[("p3s", p.c("p3s"))])
    p.barrier()


def emit_p3b(p, io, g, NT=None):
    nc = p.nc
    NB = (T // 512) if NT is None else NT
    pre = "q3"
    with contextlib.ExitStack() as ctx:
        sb = lambda n, sh, dt: ctx.enter_context(nc.sbuf_tensor(pre + n, sh, dt))
        psA = ctx.enter_context(nc.psum_tensor(pre + "psA", [128, 512], F32))
        psB = ctx.enter_context(nc.psum_tensor(pre + "psB", [128, 512], F32))
        psC = ctx.enter_context(nc.psum_tensor(pre + "psC", [128, 512], F32))
        psD = ctx.enter_context(nc.psum_tensor(pre + "psD", [128, 1536], F32))
        psU = ctx.enter_context(nc.psum_tensor(pre + "psU", [128, 1024], BF16))

        x2b = sb("x2b", [128, 8, 512], BF16)
        vfm = sb("vfm", [128, 32, 512], BF16)
        attn = [sb(f"attn{i}", [128, 16], BF16) for i in range(2)]
        attnT = sb("attnT", [16, 512], BF16)
        sel = sb("sel", [16, 2048], BF16)
        Atile = sb("Atile", [128, 16, 512], BF16)
        mixed = sb("mixed", [128, 32, 512], BF16)
        z4 = [sb(f"z4{i}", [128, 512], BF16) for i in range(4)]
        wstrip = [sb(f"ws{i}", [128, 8, 512], BF16) for i in range(2)]
        wostrip = [sb(f"wo{i}", [128, 32, 128], BF16) for i in range(2)]
        bvs = sb("bvs", [128, 32], F32)
        bos = sb("bos", [128, 8], F32)
        sbf = sb("sbf", [128, 8, 512], BF16)
        sqbf = sb("sqbf", [128, 512], BF16)
        Asc = sb("Asc", [128, 512], F32)
        Bsc = sb("Bsc", [128, 512], F32)
        tmp = sb("tmp", [128, 512], F32)
        x2f = [sb(f"x2f{i}", [128, 512], F32) for i in range(2)]
        x3f = [sb(f"x3f{i}", [128, 512], F32) for i in range(2)]
        x3tt = [sb(f"x3tt{i}", [128, 128], F32) for i in range(2)]

        for name, tile in [("l1_bv", bvs), ("l1_bo", bos)]:
            p.op("sync", lambda e, t=tile, a=io[name]: e.dma_start(out=t[:], in_=a[:]),
                 inc=["q3s"], dma=True)
        setup_ld = p.c("q3s")
        p.op("gpsimd", lambda e: e.iota(sel[:].rearrange("p (a b) -> p a b", a=16),
                                        [[-1, 16], [0, 128]], channel_multiplier=1,
                                        allow_small_or_imprecise_dtypes=True), inc=["q3g1"])
        iod = p.c("q3g1")
        p.op("vector", lambda e: e.tensor_single_scalar(sel[:], sel[:], 0.0, ALU.is_equal),
             wait=[("q3g1", iod)], inc=["q3v"])
        sel_done = p.c("q3v")

        strip_free = [0, 0]
        wo_free = [0, 0]
        sq_prev3 = [0]
        for j in range(NB):
            jc = slice(j * 512, (j + 1) * 512)
            xguard = [("q3t", p.c("q3t"))] if j > 0 else []
            p.op("gpsimd", lambda e, jc=jc: e.dma_start(out=x2b[:], in_=_chunked(io["x2"])[:, :, jc]),
                 wait=xguard, inc=["q3g"], dma=True)
            xld = p.c("q3g")
            for mg in range(8):
                par = mg % 2
                p.op("gpsimd",
                     lambda e, mg=mg, par=par:
                         e.dma_start(out=wstrip[par][:],
                                     in_=_chunked(io["l1_wv"])[:, :, mg * 512:(mg + 1) * 512]),
                     wait=[("q3t", strip_free[par])], inc=["q3g"], dma=True)
                wld = p.c("q3g")
                for m in range(4):
                    mo = mg * 4 + m
                    ps = psA if mo % 2 == 0 else psB
                    for k in range(8):
                        p.op("tensor",
                             lambda e, k=k, par=par, m=m, ps=ps:
                                 e.matmul(ps[:], wstrip[par][:, k, m * 128:(m + 1) * 128],
                                          x2b[:, k, :], start=(k == 0), stop=(k == 7),
                                          skip_group_check=True),
                             wait=([("q3g", wld), ("q3g", xld), ("q3a", p.c("q3a"))]
                                   if k == 0 else []),
                             inc=(["q3t"] if k == 7 else []))
                    tmm = p.c("q3t")
                    p.op("scalar",
                         lambda e, mo=mo, ps=ps:
                             e.activation(vfm[:, mo, :], ps[:], AF.Identity,
                                          bias=bvs[:, mo:mo + 1]),
                         wait=[("p3t" if False else "q3t", tmm),
                               ("q3s", setup_ld)], inc=["q3a"])
                strip_free[par] = p.c("q3t")
            v_done = p.c("q3a")

            for ts in range(4):
                t0 = j * 4 + ts
                ab = attn[ts % 2]
                p.op("gpsimd",
                     lambda e, t0=t0, ab=ab:
                         e.dma_start(out=ab[:], in_=io["p3attn"][t0 * 128:(t0 + 1) * 128, :]),
                     wait=[("q3t", p.c("q3t"))] if ts >= 2 else [], inc=["q3g"], dma=True)
                ald = p.c("q3g")
                p.op("tensor", lambda e, ab=ab: e.transpose(psU[0:16, 0:128], ab[:], g["identb"][:]),
                     wait=[("q3g", ald), ("q3v", p.c("q3v"))], inc=["q3t"])
                atr = p.c("q3t")
                p.op("vector",
                     lambda e, ts=ts: e.tensor_copy(attnT[:, ts * 128:(ts + 1) * 128],
                                                    psU[0:16, 0:128]),
                     wait=[("q3t", atr)], inc=["q3v"])
            attnT_done = p.c("q3v")

            for hg in range(16):
                p.op("tensor",
                     lambda e, hg=hg: e.matmul(psC[:], sel[0:16, hg * 128:(hg + 1) * 128],
                                               attnT[:], start=True, stop=True,
                                               skip_group_check=True),
                     wait=[("q3v", p.c("q3v"))], inc=["q3t"])
                amm = p.c("q3t")
                p.op("vector", lambda e, hg=hg: e.tensor_copy(Atile[:, hg, :], psC[:]),
                     wait=[("q3t", amm)], inc=["q3v"])
            a_done = p.c("q3v")
            for mi in range(32):
                h_, c_ = mi // 8, mi % 8
                for g_ in range(4):
                    p.op("vector",
                         lambda e, h_=h_, c_=c_, g_=g_:
                             e.tensor_mul(z4[g_][:], Atile[:, h_ * 4 + g_, :],
                                          vfm[:, g_ * 8 + c_, :]),
                         wait=([("q3a", v_done)] if mi == 0 and g_ == 0 else []),
                         inc=["q3v"], ss=("q3v" if g_ == 0 else None))
                p.op("vector", lambda e: e.tensor_add(z4[0][:], z4[0][:], z4[1][:]),
                     inc=["q3v"], ss="q3v")
                p.op("vector", lambda e: e.tensor_add(z4[2][:], z4[2][:], z4[3][:]),
                     inc=["q3v"], ss="q3v")
                p.op("vector", lambda e, mi=mi: e.tensor_add(
                         mixed[:, mi, :].rearrange("p (g i4) -> p i4 g", g=4),
                         z4[0][:].rearrange("p (i4 g) -> p i4 g", g=4),
                         z4[2][:].rearrange("p (i4 g) -> p i4 g", g=4)),
                     inc=["q3v"], ss="q3v")
            mixed_done = p.c("q3v")

            seq, jj = j // 4, j % 4
            colv = lambda ap: ap.rearrange("d (sq ih b i4) -> d sq ih b i4", sq=4, ih=4, b=4)
            for m in range(8):
                par = m % 2
                p.op("gpsimd",
                     lambda e, m=m, par=par:
                         e.dma_start(out=wostrip[par][:],
                                     in_=_chunked(io["l1_wo"])[:, :, m * 128:(m + 1) * 128]),
                     wait=[("q3t", wo_free[par])], inc=["q3g"], dma=True)
                wld = p.c("q3g")
                p.op("sync",
                     lambda e, m=m, par=par, seq=seq, jj=jj:
                         e.dma_start(out=x2f[par][:].rearrange("p (ih i4) -> p ih i4", ih=4),
                                     in_=colv(io["x2"])[m * 128:(m + 1) * 128, seq, :, jj, :]),
                     wait=[("q3v", p.c("q3v"))] if m >= 2 else [], inc=["q3s"], dma=True)
                lx = p.c("q3s")
                ps = psA if m % 2 == 0 else psB
                for ih in range(4):
                    for k in range(32):
                        g_, c_ = k // 8, k % 8
                        p.op("tensor",
                             lambda e, k=k, par=par, ps=ps, ih=ih, g_=g_, c_=c_:
                                 e.matmul(ps[:, ih * 128:(ih + 1) * 128],
                                          wostrip[par][:, k, :],
                                          mixed[:, ih * 8 + c_, g_ * 128:(g_ + 1) * 128],
                                          start=(k == 0), stop=(k == 31),
                                          skip_group_check=True),
                             wait=([("q3g", wld), ("q3v", p.c("q3v"))]
                                   if ih == 0 and k == 0 else []),
                             inc=(["q3t"] if ih == 3 and k == 31 else []))
                tmm = p.c("q3t")
                wo_free[par] = tmm
                p.op("vector",
                     lambda e, m=m, par=par, ps=ps:
                         e.scalar_tensor_tensor(sbf[:, m, :], ps[:], bos[:, m:m + 1],
                                                x2f[par][:], ALU.add, ALU.add),
                     wait=[("q3s", lx), ("q3t", tmm)], inc=["q3v"])
                svm = p.c("q3v")
                p.op("scalar",
                     lambda e, m=m: e.activation(sqbf[:], sbf[:, m, :], AF.Square),
                     wait=[("q3v", svm), ("q3t", sq_prev3[0])], inc=["q3a"])
                sqm = p.c("q3a")
                p.op("tensor",
                     lambda e, m=m: e.matmul(psD[:, 0:512], g["ones128"][:], sbf[:, m, :],
                                             start=(m == 0), stop=(m == 7), skip_group_check=True),
                     wait=[("q3v", svm)] + ([("q3v", p.c("q3v"))] if m == 0 else []),
                     inc=(["q3t"] if m == 7 else []))
                p.op("tensor",
                     lambda e, m=m: e.matmul(psD[:, 512:1024], g["ones128"][:], sqbf[:],
                                             start=(m == 0), stop=(m == 7), skip_group_check=True),
                     wait=[("q3a", sqm)], inc=["q3t"])
                sq_prev3[0] = p.c("q3t")
            tstat = p.c("q3t")
            p.op("vector", lambda e: e.tensor_scalar(Asc[:], psD[:, 0:512], 1.0 / D, None, ALU.mult),
                 wait=[("q3t", tstat)], inc=["q3v"])
            p.op("vector", lambda e: e.tensor_scalar(tmp[:], psD[:, 512:1024], 1.0 / D, None, ALU.mult),
                 inc=["q3v"])
            p.op("vector", lambda e: e.tensor_tensor(Bsc[:], Asc[:], Asc[:], ALU.mult),
                 inc=["q3v"], ss="q3v")
            p.op("vector", lambda e: e.tensor_sub(tmp[:], tmp[:], Bsc[:]), inc=["q3v"], ss="q3v")
            p.op("vector", lambda e: e.tensor_scalar(tmp[:], tmp[:], EPS, None, ALU.add),
                 inc=["q3v"], ss="q3v")
            p.op("vector", lambda e: e.reciprocal(tmp[:], tmp[:]), inc=["q3v"], ss="q3v")
            md = p.c("q3v")
            p.op("scalar", lambda e: e.activation(tmp[:], tmp[:], AF.Sqrt),
                 wait=[("q3v", md)], inc=["q3a"], ss="q3a")
            rsd = p.c("q3a")
            p.op("vector", lambda e: e.scalar_tensor_tensor(Bsc[:], Asc[:], -1.0, tmp[:],
                                                            ALU.mult, ALU.mult),
                 wait=[("q3a", rsd)], inc=["q3v"], ss="q3v")
            for m in range(8):
                buf = x3f[m % 2]
                stg = [("q3s", p.c("q3s"))] if m >= 2 else []
                p.op("vector",
                     lambda e, m=m, buf=buf: e.tensor_tensor(buf[:], sbf[:, m, :], tmp[:], ALU.mult),
                     wait=stg, inc=["q3v"], ss="q3v")
                p.op("vector", lambda e, m=m, buf=buf: e.tensor_add(buf[:], buf[:], Bsc[:]),
                     inc=["q3v"], ss="q3v")
                vd = p.c("q3v")
                p.op("sync",
                     lambda e, m=m, buf=buf, seq=seq, jj=jj:
                         e.dma_start(out=colv(io["x3"])[m * 128:(m + 1) * 128, seq, :, jj, :],
                                     in_=buf[:].rearrange("p (ih i4) -> p ih i4", ih=4)),
                     wait=[("q3v", vd)], inc=["q3s"], dma=True)
                for ts in range(4):
                    p.op("tensor",
                         lambda e, ts=ts, buf=buf:
                             e.transpose(psD[:, 1024:1152],
                                         buf[:, ts * 128:(ts + 1) * 128], g["identf32"][:]),
                         wait=[("q3v", p.c("q3v"))], inc=["q3t"])
                    trd = p.c("q3t")
                    tb = x3tt[ts % 2]
                    p.op("vector", lambda e, tb=tb: e.tensor_copy(tb[:], psD[:, 1024:1152]),
                         wait=[("q3t", trd), ("q3s", p.c("q3s"))], inc=["q3v"])
                    cvd2 = p.c("q3v")
                    row0 = seq * 2048 + ts * 512 + jj * 128
                    p.op("sync",
                         lambda e, m=m, row0=row0, tb=tb:
                             e.dma_start(out=io["x3t"][row0:row0 + 128,
                                                       m * 128:(m + 1) * 128], in_=tb[:]),
                         wait=[("q3v", cvd2)], inc=["q3s"], dma=True)
        p.op("sync", lambda e: None, wait=[("q3s", p.c("q3s"))])
    p.barrier()


def emit_p4(p, io, g, NT=None, skip_ln=False, coef_out=None):
    nc = p.nc
    NTILE = (T // 128) if NT is None else NT * 4
    NB = NTILE // 4
    pre = "p4"
    with contextlib.ExitStack() as ctx:
        sb = lambda n, sh, dt: ctx.enter_context(nc.sbuf_tensor(pre + n, sh, dt))
        psD = ctx.enter_context(nc.psum_tensor(pre + "psD", [128, 2048], F32))
        psA = ctx.enter_context(nc.psum_tensor(pre + "psA", [128, 512], F32))
        psB = ctx.enter_context(nc.psum_tensor(pre + "psB", [128, 512], F32))
        psC = ctx.enter_context(nc.psum_tensor(pre + "psC", [128, 512], F32))
        coef = sb("coef", [128, NTILE * 8], F32)
        gdbg = sb("gdbg", [128, 32], F32) if coef_out else None

        with contextlib.ExitStack() as gtx:
            sbg = lambda n, sh, dt: gtx.enter_context(nc.sbuf_tensor(pre + n, sh, dt))
            x3g = sbg("x3g", [128, 8, 512], F32)
            wgf = sbg("wgf", [128, 8, 8], F32)
            bgf = sbg("bgf", [1, 8], F32)
            lg = sbg("lg", [128, 8], F32)
            mx = sbg("mx", [128, 1], F32)
            negmx = sbg("negmx", [128, 1], F32)
            ex = sbg("ex", [128, 8], F32)
            ssum = sbg("ssum", [128, 1], F32)
            p.op("sync", lambda e: e.dma_start(out=wgf[:], in_=_chunked(io["wg"])[:]),
                 inc=["p4s"], dma=True)
            p.op("sync", lambda e: e.dma_start(out=bgf[:], in_=io["bg"][:]),
                 inc=["p4s"], dma=True)
            wld = p.c("p4s")
            for j in range(NB):
                jc = slice(j * 512, (j + 1) * 512)
                guard = [("p4t", p.c("p4t"))] if j > 0 else []
                p.op("sync", lambda e, jc=jc: e.dma_start(out=x3g[:], in_=_chunked(io["x3"])[:, :, jc]),
                     wait=guard, inc=["p4s"], dma=True)
                xld = p.c("p4s")
                for ts in range(4):
                    t0 = j * 4 + ts
                    tsl = slice(ts * 128, (ts + 1) * 128)
                    for k in range(8):
                        p.op("tensor",
                             lambda e, k=k, tsl=tsl: e.matmul(psC[:, 0:8], x3g[:, k, tsl],
                                                              wgf[:, k, :], start=(k == 0),
                                                              stop=False, skip_group_check=True),
                             wait=([("p4s", xld), ("p4v", p.c("p4v"))] if k == 0 else []),
                             inc=[])
                    p.op("tensor",
                         lambda e: e.matmul(psC[:, 0:8], g["onesf1"][:], bgf[:],
                                            start=False, stop=True, skip_group_check=True),
                         inc=["p4t"])
                    gm = p.c("p4t")
                    p.op("vector", lambda e: e.tensor_copy(lg[:], psC[:, 0:8]),
                         wait=[("p4t", gm)], inc=["p4v"])
                    p.op("vector", lambda e: e.reduce_max(mx[:], lg[:], axis=AX.X), inc=["p4v"],
                         ss="p4v")
                    p.op("vector", lambda e: e.tensor_scalar(negmx[:], mx[:], -1.0, None, ALU.mult),
                         inc=["p4v"], ss="p4v")
                    vd = p.c("p4v")
                    p.op("scalar", lambda e: e.activation(ex[:], lg[:], AF.Exp, bias=negmx[:],
                                                          accum_out=ssum[:]),
                         wait=[("p4v", vd)], inc=["p4a"])
                    ad = p.c("p4a")
                    p.op("vector", lambda e: e.reciprocal(ssum[:], ssum[:]),
                         wait=[("p4a", ad)], inc=["p4v"], ss="p4v")
                    if gdbg is not None and t0 == 0:
                        p.op("vector", lambda e: e.tensor_copy(gdbg[:, 0:8], lg[:]), inc=["p4v"])
                        p.op("vector", lambda e: e.tensor_copy(gdbg[:, 8:9], mx[:]), inc=["p4v"])
                        p.op("vector", lambda e: e.tensor_copy(gdbg[:, 9:10], negmx[:]), inc=["p4v"])
                        p.op("vector", lambda e: e.tensor_copy(gdbg[:, 10:11], ssum[:]), inc=["p4v"])
                        p.op("vector", lambda e: e.tensor_copy(gdbg[:, 11:19], ex[:]), inc=["p4v"])
                    for ee in range(8):
                        p.op("vector",
                             lambda e, ee=ee, t0=t0:
                                 e.tensor_scalar(coef[:, t0 * 8 + ee:t0 * 8 + ee + 1],
                                                 lg[:, ee:ee + 1], mx[:], ssum[:],
                                                 ALU.is_equal, ALU.mult),
                             inc=["p4v"], ss=("p4v" if ee == 0 else None))
            gate_done_v = p.c("p4v")
            gate_done_t = p.c("p4t")

        for m in range(8):
            p.op("sync",
                 lambda e, m=m: e.dma_start(out=io["out"][m * (T // 8):(m + 1) * (T // 8), :],
                                            in_=io["x3t"][m * (T // 8):(m + 1) * (T // 8), :]),
                 inc=["p4s"], dma=True)
        init_done = p.c("p4s")

        w1e = sb("w1e", [128, 8, DH], BF16)
        w2e = sb("w2e", [128, 32, D], BF16)
        b1s = sb("b1s", [128, 32], F32)
        b2r = sb("b2r", [1, D], BF16)
        x3b = sb("x3b", [128, 8, 512], BF16)
        hbuf = sb("hbuf", [128, 32, 512], BF16)
        of32 = [sb(f"of{i}", [128, D], F32) for i in range(2)]
        for e_i in range(NEXP):
            g1 = [("p4t", p.c("p4t")), ("p4v", gate_done_v)] if e_i > 0 else [("p4v", gate_done_v)]
            for k in range(8):
                p.op("gpsimd",
                     lambda e, e_i=e_i, k=k: e.dma_start(out=w1e[:, k, :],
                                                         in_=_chunked(io["m_w1"][e_i])[:, k, :]),
                     wait=(g1 if k == 0 else []), inc=["p4g"], dma=True)
            for c4 in range(8):
                p.op("gpsimd",
                     lambda e, e_i=e_i, c4=c4: e.dma_start(
                         out=w2e[:, c4 * 4:(c4 + 1) * 4, :],
                         in_=_chunked(io["m_w2"][e_i])[:, c4 * 4:(c4 + 1) * 4, :]),
                     inc=["p4g"], dma=True)
            p.op("gpsimd", lambda e, e_i=e_i: e.dma_start(out=b1s[:], in_=io["m_b1"][e_i]),
                 inc=["p4g"], dma=True)
            p.op("gpsimd", lambda e, e_i=e_i: e.dma_start(out=b2r[:], in_=io["m_b2"][e_i]),
                 inc=["p4g"], dma=True)
            wld = p.c("p4g")
            for j in range(NB):
                jc = slice(j * 512, (j + 1) * 512)
                xguard = [("p4t", p.c("p4t"))] if (e_i > 0 or j > 0) else []
                p.op("gpsimd", lambda e, jc=jc: e.dma_start(out=x3b[:], in_=_chunked(io["x3"])[:, :, jc]),
                     wait=xguard, inc=["p4g"], dma=True)
                xld = p.c("p4g")
                for mh in range(32):
                    ps = psA if mh % 2 == 0 else psB
                    for k in range(8):
                        p.op("tensor",
                             lambda e, mh=mh, k=k, ps=ps:
                                 e.matmul(ps[:], w1e[:, k, mh * 128:(mh + 1) * 128], x3b[:, k, :],
                                          start=(k == 0), stop=(k == 7), skip_group_check=True),
                             wait=([("p4g", xld), ("p4a", p.c("p4a") - (1 if mh >= 2 else 0))]
                                   if k == 0 else []),
                             inc=(["p4t"] if k == 7 else []))
                    hm = p.c("p4t")
                    p.op("scalar",
                         lambda e, mh=mh, ps=ps: e.activation(hbuf[:, mh, :], ps[:], AF.Silu,
                                                              bias=b1s[:, mh:mh + 1]),
                         wait=[("p4t", hm)], inc=["p4a"])
                hdone = p.c("p4a")
                for ts in range(4):
                    t0 = j * 4 + ts
                    tsl = slice(ts * 128, (ts + 1) * 128)
                    pso = psD[:, 0:1024] if ts % 2 == 0 else psD[:, 1024:2048]
                    for nh in range(2):
                        for k in range(32):
                            p.op("tensor",
                                 lambda e, k=k, tsl=tsl, nh=nh, pso=pso:
                                     e.matmul(pso[:, nh * 512:(nh + 1) * 512],
                                              hbuf[:, k, tsl], w2e[:, k, nh * 512:(nh + 1) * 512],
                                              start=(k == 0), stop=False, skip_group_check=True),
                                 wait=([("p4a", hdone), ("p4v", p.c("p4v") - (1 if (e_i > 0 or j > 0 or ts >= 2) else 0))]
                                       if k == 0 and nh == 0 else []),
                                 inc=[])
                        p.op("tensor",
                             lambda e, tsl=tsl, nh=nh, pso=pso:
                                 e.matmul(pso[:, nh * 512:(nh + 1) * 512], g["ones1"][0:1, 0:128],
                                          b2r[:, nh * 512:(nh + 1) * 512],
                                          start=False, stop=True, skip_group_check=True),
                             inc=(["p4t"] if nh == 1 else []))
                    om = p.c("p4t")
                    ob = of32[ts % 2]
                    stguard = [("p4g", p.c("p4g"))] if (e_i > 0 or j > 0 or ts >= 2) else []
                    p.op("vector",
                         lambda e, t0=t0, e_i=e_i, pso=pso, ob=ob:
                             e.tensor_scalar(ob[:], pso[:], coef[:, t0 * 8 + e_i:t0 * 8 + e_i + 1],
                                             None, ALU.mult),
                         wait=[("p4t", om)] + stguard, inc=["p4v"])
                    ovd = p.c("p4v")
                    p.op("gpsimd",
                         lambda e, t0=t0, ob=ob:
                             e.dma_start(out=io["out"][t0 * 128:(t0 + 1) * 128, :], in_=ob[:],
                                         accum_op=ALU.add),
                         wait=[("p4v", ovd)] + ([("p4s", init_done)] if e_i == 0 and j == 0 else []),
                         inc=["p4g"], dma=True)
        acc_done = p.c("p4g")
        if coef_out is not None:
            p.op("sync", lambda e: e.dma_start(out=io[coef_out][:, 0:NTILE * 8], in_=coef[:]),
                 wait=[("p4v", p.c("p4v"))], inc=["p4s"], dma=True)
            p.op("sync", lambda e: e.dma_start(out=io["gdbgo"][:], in_=gdbg[:]),
                 inc=["p4s"], dma=True)
        if skip_ln:
            p.op("sync", lambda e: None, wait=[("p4g", acc_done), ("p4s", p.c("p4s"))])
            p.barrier()
            return

        yt = [sb(f"yt{i}", [128, D], F32) for i in range(2)]
        yq = sb("yq", [128, D], F32)
        s1 = sb("sA", [128, 1], F32)
        s2 = sb("sB", [128, 1], F32)
        s3 = sb("sC", [128, 1], F32)
        for t0 in range(NTILE):
            buf = yt[t0 % 2]
            lguard = [("p4s", p.c("p4s"))] if t0 >= 2 else [("p4g", acc_done)]
            p.op("sync", lambda e, t0=t0, buf=buf:
                     e.dma_start(out=buf[:], in_=io["out"][t0 * 128:(t0 + 1) * 128, :]),
                 wait=lguard, inc=["p4s"], dma=True)
            yld = p.c("p4s")
            p.op("vector", lambda e, buf=buf: e.reduce_sum(s1[:], buf[:], axis=AX.X),
                 wait=[("p4s", yld)], inc=["p4v"], ss="p4v")
            vd = p.c("p4v")
            p.op("scalar", lambda e, buf=buf: e.activation(yq[:], buf[:], AF.Square, accum_out=s2[:]),
                 wait=[("p4s", yld), ("p4v", p.c("p4v"))], inc=["p4a"])
            aq = p.c("p4a")
            p.op("vector", lambda e: e.tensor_scalar(s1[:], s1[:], 1.0 / D, None, ALU.mult),
                 inc=["p4v"], ss="p4v")
            p.op("vector", lambda e: e.tensor_scalar(s2[:], s2[:], 1.0 / D, None, ALU.mult),
                 wait=[("p4a", aq)], inc=["p4v"], ss="p4v")
            p.op("vector", lambda e: e.tensor_tensor(s3[:], s1[:], s1[:], ALU.mult), inc=["p4v"], ss="p4v")
            p.op("vector", lambda e: e.tensor_sub(s2[:], s2[:], s3[:]), inc=["p4v"], ss="p4v")
            p.op("vector", lambda e: e.tensor_scalar(s2[:], s2[:], EPS, None, ALU.add), inc=["p4v"], ss="p4v")
            p.op("vector", lambda e: e.reciprocal(s2[:], s2[:]), inc=["p4v"], ss="p4v")
            rv = p.c("p4v")
            p.op("scalar", lambda e: e.activation(s2[:], s2[:], AF.Sqrt),
                 wait=[("p4v", rv)], inc=["p4a"])
            rs = p.c("p4a")
            p.op("vector", lambda e, buf=buf: e.tensor_scalar(buf[:], buf[:], s1[:], s2[:],
                                                              ALU.subtract, ALU.mult),
                 wait=[("p4a", rs)], inc=["p4v"], ss="p4v")
            nd = p.c("p4v")
            p.op("sync", lambda e, t0=t0, buf=buf:
                     e.dma_start(out=io["out"][t0 * 128:(t0 + 1) * 128, :], in_=buf[:]),
                 wait=[("p4v", nd)], inc=["p4s"], dma=True)
        p.op("sync", lambda e: None, wait=[("p4s", p.c("p4s"))])
    p.barrier()


def emit_globals(p, g):
    nc = p.nc
    p.op("gpsimd", lambda e: e.iota(g["iotaf"][:], [[-1, 128]], channel_multiplier=1,
                                    allow_small_or_imprecise_dtypes=True), inc=["gg"])
    iot = p.c("gg")
    p.op("vector", lambda e: e.tensor_single_scalar(g["identb"][:], g["iotaf"][:], 0.0, ALU.is_equal),
         wait=[("gg", iot)], inc=["gv"])
    p.op("vector", lambda e: e.tensor_single_scalar(g["identf32"][:], g["iotaf"][:], 0.0, ALU.is_equal),
         inc=["gv"])
    p.op("vector", lambda e: e.memset(g["ones128"][:], 1.0), inc=["gv"])
    p.op("vector", lambda e: e.memset(g["ones1"][:], 1.0), inc=["gv"])
    p.op("vector", lambda e: e.memset(g["onesf1"][:], 1.0), inc=["gv"])
    idone = p.c("gv")
    p.op("vector", lambda e: e.memset(g["iotaf"][:], 0.0), inc=["gv"])
    zdone = p.c("gv")
    p.op("gpsimd", lambda e: e.affine_select(g["maskf"][:], g["iotaf"][:], [[-1, 128]],
                                             ALU.is_ge, -30000.0, channel_multiplier=1),
         wait=[("gv", zdone)], inc=["gg"])
    p.barrier()


def build_globals(nc, ctx):
    g = {}
    g["iotaf"] = ctx.enter_context(nc.sbuf_tensor("g_iotaf", [128, 128], F32))
    g["identb"] = ctx.enter_context(nc.sbuf_tensor("g_identb", [128, 128], BF16))
    g["ones128"] = ctx.enter_context(nc.sbuf_tensor("g_ones128", [128, 128], BF16))
    g["ones1"] = ctx.enter_context(nc.sbuf_tensor("g_ones1", [1, 512], BF16))
    g["onesf1"] = ctx.enter_context(nc.sbuf_tensor("g_onesf1", [1, 128], F32))
    g["maskf"] = ctx.enter_context(nc.sbuf_tensor("g_maskf", [128, 128], F32))
    g["identf32"] = ctx.enter_context(nc.sbuf_tensor("g_identf32", [128, 128], F32))
    return g


def build_program(phases=("p2",), p2_tb=512, p2_nt=None):
    nc = bass.Bass("TRN2", target_bir_lowering=False)
    ctx = contextlib.ExitStack()
    io = {}

    def dram(name, shape, kind):
        io[name] = nc.dram_tensor(name, shape, F32, kind=kind).ap()

    first = phases[0]
    stream_kind = {}
    order = ["x0", "x1", "x2", "x3"]
    pin = {"p1": "x0", "p2": "x1", "p3": "x2", "p4": "x3"}
    pout = {"p1": "x1", "p2": "x2", "p3": "x3", "p4": None}
    needed = set()
    for ph in phases:
        needed.add(pin[ph])
        if pout[ph]:
            needed.add(pout[ph])
    for sname in order:
        if sname not in needed:
            continue
        if sname == pin[first]:
            dram(sname, [D, T], "ExternalInput")
        elif any(pout[ph] == sname for ph in phases) and not any(pin[ph] == sname for ph in phases):
            dram(sname, [D, T], "ExternalOutput")
        else:
            dram(sname, [D, T], "Internal")
    if "p3" in phases:
        dram("x3t", [T, D], "Internal" if "p4" in phases else "ExternalOutput")
        dram("p3attn", [T, 16], "Internal" if not P4_DEBUG[0] else "ExternalOutput")
        if P4_DEBUG[0]:
            dram("p3dbg", [128, 112], "ExternalOutput")
    if "p4" in phases:
        if "p3" not in phases:
            dram("x3t", [T, D], "ExternalInput")
        dram("out", [T, D], "ExternalOutput")
        if P4_DEBUG[0]:
            dram("coefdbg", [128, (T // 128) * 8], "ExternalOutput")
            dram("gdbgo", [128, 32], "ExternalOutput")

    if "p1" in phases:
        for n, sh in [("l0_wq", [D, D]), ("l0_wk", [D, D]), ("l0_wv", [D, D]),
                      ("l0_wo", [D, D]), ("l0_bq", [128, 8]), ("l0_bk", [128, 8]),
                      ("l0_bv", [1, D]), ("l0_bo", [128, 8])]:
            dram(n, sh, "ExternalInput")
    if "p2" in phases:
        for n, sh in [("w1T", [D, DH]), ("w2T", [D, DH]), ("h1T", [DH, DH]),
                      ("h2T", [DH, DH]), ("w3T", [DH, D]),
                      ("w1_s", [128, 32]), ("w2_s", [128, 32]), ("h1_s", [128, 32]),
                      ("h2_s", [128, 32]), ("w3_s", [128, 8])]:
            dram(n, sh, "ExternalInput")
    if "p3" in phases:
        for n, sh in [("l1_wq", [D, 4 * D]), ("l1_wk", [D, 4 * D]), ("l1_wv", [D, 4 * D]),
                      ("l1_wo", [4 * D, D]), ("l1_ph", [1, 4 * D]), ("l1_bq", [128, 32]),
                      ("l1_bk", [128, 32]), ("l1_bv", [128, 32]), ("l1_bo", [128, 8])]:
            dram(n, sh, "ExternalInput")
    if "p4" in phases:
        for n, sh in [("wg", [D, NEXP]), ("bg", [1, NEXP]), ("m_w1", [NEXP, D, DH]),
                      ("m_b1", [NEXP, 128, 32]), ("m_w2", [NEXP, DH, D]),
                      ("m_b2", [NEXP, 1, D])]:
            dram(n, sh, "ExternalInput")

    p = P(nc, ctx)
    g = build_globals(nc, ctx)
    with nc.Block() as block:
        emit_globals(p, g)
        for ph in phases:
            if ph == "p2":
                emit_p2(p, io, g, TB=p2_tb, NT=p2_nt)
            elif ph == "p1":
                emit_p1(p, io, g, NSQ=(None if p2_nt is None else max(1, p2_nt // 4)))
            elif ph == "p3":
                emit_p3(p, io, g, NT=p2_nt)
            elif ph == "p4":
                emit_p4(p, io, g, NT=p2_nt, skip_ln=P4_DEBUG[0],
                        coef_out="coefdbg" if P4_DEBUG[0] else None)

        @block.tensor
        def _(e):
            for f in p.streams["tensor"]:
                f(e)

        @block.vector
        def _(e):
            for f in p.streams["vector"]:
                f(e)

        @block.scalar
        def _(e):
            for f in p.streams["scalar"]:
                f(e)

        @block.sync
        def _(e):
            for f in p.streams["sync"]:
                f(e)

        @block.gpsimd
        def _(e):
            for f in p.streams["gpsimd"]:
                f(e)
    ctx.close()
    return nc


def stage_shared(inputs):
    a = {}
    c = np.ascontiguousarray
    if "L0_attn_Wq" in inputs:
        a["l0_wq"] = c(inputs["L0_attn_Wq"].transpose(1, 0, 2).reshape(D, D))
        a["l0_wk"] = c(inputs["L0_attn_Wk"].transpose(1, 0, 2).reshape(D, D))
        a["l0_wv"] = c(inputs["L0_attn_Wv"].transpose(1, 0, 2).reshape(D, D))
        a["l0_wo"] = c(inputs["L0_attn_Wo"])
        a["l0_bq"] = c(inputs["L0_attn_bq"].reshape(-1).reshape(8, 128).T)
        a["l0_bk"] = c(inputs["L0_attn_bk"].reshape(-1).reshape(8, 128).T)
        a["l0_bv"] = c(inputs["L0_attn_bv"].reshape(1, D))
        a["l0_bo"] = c(inputs["L0_attn_bo"].reshape(8, 128).T)
    if "L0_ffn_w1_w" in inputs:
        a["w1T"] = c(inputs["L0_ffn_w1_w"].T)
        a["w2T"] = c(inputs["L0_ffn_w2_w"].T)
        a["h1T"] = c(inputs["L0_ffn_h1_w"].T)
        a["h2T"] = c(inputs["L0_ffn_h2_w"].T)
        a["w3T"] = c(inputs["L0_ffn_w3_w"].T)
        a["w1_s"] = c(inputs["L0_ffn_w1_s"].reshape(32, 128).T)
        a["w2_s"] = c(inputs["L0_ffn_w2_s"].reshape(32, 128).T)
        a["h1_s"] = c(inputs["L0_ffn_h1_s"].reshape(32, 128).T)
        a["h2_s"] = c(inputs["L0_ffn_h2_s"].reshape(32, 128).T)
        a["w3_s"] = c(inputs["L0_ffn_w3_s"].reshape(8, 128).T)
    if "L1_attn_Wq" in inputs:
        a["l1_wq"] = c(inputs["L1_attn_Wq"])
        a["l1_wk"] = c(inputs["L1_attn_Wk"])
        a["l1_wv"] = c(inputs["L1_attn_Wv"])
        a["l1_wo"] = c(inputs["L1_attn_Wo"])
        a["l1_ph"] = c(inputs["L1_attn_phase"].reshape(1, 4 * D))
        a["l1_bq"] = c(inputs["L1_attn_bq"].reshape(32, 128).T)
        a["l1_bk"] = c(inputs["L1_attn_bk"].reshape(32, 128).T)
        a["l1_bv"] = c(inputs["L1_attn_bv"].reshape(32, 128).T)
        a["l1_bo"] = c(inputs["L1_attn_bo"].reshape(8, 128).T)
    if "L1_ffn_Wg" in inputs:
        a["wg"] = c(inputs["L1_ffn_Wg"])
        a["bg"] = c(inputs["L1_ffn_bg"].reshape(1, NEXP))
        a["m_w1"] = c(inputs["L1_ffn_W1"])
        a["m_b1"] = c(inputs["L1_ffn_b1"].reshape(NEXP, 32, 128).transpose(0, 2, 1))
        a["m_w2"] = c(inputs["L1_ffn_W2"])
        a["m_b2"] = c(inputs["L1_ffn_b2"].reshape(NEXP, 1, D))
    return a


def stage_x_fm(x):
    return np.ascontiguousarray(x.transpose(2, 0, 1).reshape(D, -1))


FUSED = True

_P1KEYS = ["l0_wq", "l0_wk", "l0_wv", "l0_wo", "l0_bq", "l0_bk", "l0_bv", "l0_bo"]
_P2KEYS = ["w1T", "w2T", "h1T", "h2T", "w3T", "w1_s", "w2_s", "h1_s", "h2_s", "w3_s"]
_P3KEYS = ["l1_wq", "l1_wk", "l1_wv", "l1_wo", "l1_ph", "l1_bq", "l1_bk", "l1_bv", "l1_bo"]
_P4KEYS = ["wg", "bg", "m_w1", "m_b1", "m_w2", "m_b2"]


def kernel(**inputs):
    shared = stage_shared(inputs)
    x = inputs["L0_attn_x"]
    x0s = [stage_x_fm(x[c * NSEQ:(c + 1) * NSEQ]) for c in range(NCORES)]
    cores = list(range(NCORES))
    if FUSED:
        nc = build_program(phases=("p1", "p2", "p3", "p4"))
        in_maps = [dict(shared, x0=x0s[c]) for c in cores]
        res = run_bass_kernel_spmd(nc, in_maps, core_ids=cores)
        return np.concatenate([r["out"].reshape(NSEQ, S, D) for r in res.results], axis=0)
    nc1 = build_program(phases=("p1",))
    r1 = run_bass_kernel_spmd(
        nc1, [{k: shared[k] for k in _P1KEYS} | {"x0": x0s[c]} for c in cores], core_ids=cores)
    x1s = [r["x1"] for r in r1.results]
    nc2 = build_program(phases=("p2",))
    r2 = run_bass_kernel_spmd(
        nc2, [{k: shared[k] for k in _P2KEYS} | {"x1": x1s[c]} for c in cores], core_ids=cores)
    x2s = [r["x2"] for r in r2.results]
    nc3 = build_program(phases=("p3",))
    r3 = run_bass_kernel_spmd(
        nc3, [{k: shared[k] for k in _P3KEYS} | {"x2": x2s[c]} for c in cores], core_ids=cores)
    nc4 = build_program(phases=("p4",))
    r4 = run_bass_kernel_spmd(
        nc4, [{k: shared[k] for k in _P4KEYS} |
              {"x3": r3.results[c]["x3"], "x3t": r3.results[c]["x3t"]} for c in cores],
        core_ids=cores)
    return np.concatenate([r["out"].reshape(NSEQ, S, D) for r in r4.results], axis=0)
